# version 11
# baseline (speedup 1.0000x reference)
import numpy as np
import ml_dtypes
from contextlib import ExitStack

import concourse.bass as bass
import concourse.mybir as mybir
from concourse.bass_utils import run_bass_kernel_spmd

F32 = mybir.dt.float32
BF16 = mybir.dt.bfloat16
AF = mybir.ActivationFunctionType
ALU = mybir.AluOpType
AX = mybir.AxisListType

NCORES = 8
SEQ = 2
S = 2048
H = 2048
NH = 16
NKV = 8
HD = 128
WIN = 1024
NBLK = 16
KTW = WIN // 128
DH = 5632
NJ = 44
NKD = 16
GT = 512
NG = (SEQ * S) // GT
NTT = GT // 128
INV_SQRT_HD = float(1.0 / np.sqrt(np.float32(HD)))
NW_ELEMS = float(DH * H)

ENGINES = ("sync", "gpsimd", "scalar", "vector", "tensor")
DMA_ENGINES = ("sync", "gpsimd", "scalar")


class Buf:
    __slots__ = ("name", "lw", "readers", "dma_sem", "dma_cnt", "psum")

    def __init__(self, name, psum=False):
        self.name = name
        self.lw = None
        self.readers = []
        self.dma_sem = None
        self.dma_cnt = 0
        self.psum = psum


class Op:
    __slots__ = ("eng", "fn", "deps", "marked", "idx", "cnt", "dma_ev")

    def __init__(self, eng, fn, idx):
        self.eng = eng
        self.fn = fn
        self.deps = []
        self.marked = False
        self.idx = idx
        self.cnt = None
        self.dma_ev = None


class Prog:
    def __init__(self, nc, es):
        self.nc = nc
        self.es = es
        self.sems = {e: es.enter_context(nc.semaphore(f"sem_{e}")) for e in ENGINES}
        self.ops = {e: [] for e in ENGINES}
        self.waited_op = {e: {f: -1 for f in ENGINES} for e in ENGINES}
        self.waited_sem = {e: {} for e in ENGINES}
        self.dma_bufs = []
        self.sem_objs = {}

    def buf_sem(self, buf):
        if buf.dma_sem is None:
            buf.dma_sem = self.es.enter_context(
                self.nc.semaphore(f"ds_{buf.name}"))
            self.dma_bufs.append(buf)
            self.sem_objs[id(buf.dma_sem)] = buf.dma_sem
        return buf.dma_sem

    def op(self, eng, fn, reads=(), writes=(), dma_buf=None):
        o = Op(eng, fn, len(self.ops[eng]))
        deps = []
        for b in reads:
            if b.lw is not None:
                deps.append(b.lw)
            if b.psum and eng in ("scalar", "vector"):
                other = "vector" if eng == "scalar" else "scalar"
                for ev in reversed(b.readers):
                    if ev[0] == "op" and ev[1].eng == other:
                        deps.append(ev)
                        break
        for b in writes:
            if b.lw is not None:
                deps.append(b.lw)
            deps.extend(b.readers)
        for ev in deps:
            if ev[0] == "op":
                d = ev[1]
                if d.dma_ev is not None:
                    sem, val = d.dma_ev[1], d.dma_ev[2]
                    if self.waited_sem[eng].get(id(sem), -1) < val:
                        o.deps.append(("sem", sem, val))
                        self.waited_sem[eng][id(sem)] = val
                else:
                    if d.eng == eng == "tensor":
                        continue
                    if self.waited_op[eng][d.eng] < d.idx:
                        d.marked = True
                        o.deps.append(("op", d))
                        self.waited_op[eng][d.eng] = d.idx
            else:
                _, sem, val = ev
                if self.waited_sem[eng].get(id(sem), -1) < val:
                    o.deps.append(ev)
                    self.waited_sem[eng][id(sem)] = val
        if dma_buf is not None:
            assert eng in DMA_ENGINES
            sem = self.buf_sem(dma_buf)
            dma_buf.dma_cnt += 1
            o.dma_ev = ("sem", sem, 16 * dma_buf.dma_cnt)
        self.ops[eng].append(o)
        ev = ("op", o)
        for b in reads:
            b.readers.append(ev)
        for b in writes:
            b.lw = ev
            b.readers = []
        return o

    def dma(self, eng, out_ap, in_ap, buf, reads=(), writes=()):
        def fn(E, o=out_ap, i=in_ap):
            return E.dma_start(out=o, in_=i)
        return self.op(eng, fn, reads=reads, writes=writes, dma_buf=buf)

    last_anchor = None
    BARRIER_MODE = "anchor"

    def barrier_direct(self):
        snap = {}
        for e in ENGINES:
            last = None
            for o in reversed(self.ops[e]):
                if o.fn is not None and o.dma_ev is None:
                    last = o
                    break
            snap[e] = last
        semsnap = [(b.dma_sem, 16 * b.dma_cnt) for b in self.dma_bufs if b.dma_cnt > 0]
        for e in ENGINES:
            w = Op(e, None, len(self.ops[e]))
            for f in ENGINES:
                d = snap[f]
                if f == e or d is None:
                    continue
                if self.waited_op[e][f] < d.idx:
                    d.marked = True
                    w.deps.append(("op", d))
                    self.waited_op[e][f] = d.idx
            for sem, val in semsnap:
                if self.waited_sem[e].get(id(sem), -1) < val:
                    w.deps.append(("sem", sem, val))
                    self.waited_sem[e][id(sem)] = val
            self.ops[e].append(w)

    def barrier(self, anchor_ap):
        if self.BARRIER_MODE == "direct":
            return self.barrier_direct()
        return self.barrier_anchor(anchor_ap)

    def barrier_anchor(self, anchor_ap):
        def anchor_fn(E, ap=anchor_ap):
            return E.tensor_scalar(ap, ap, 1.0, None, ALU.mult)
        o = Op("vector", anchor_fn, len(self.ops["vector"]))
        if self.last_anchor is not None and \
                self.waited_op["vector"]["vector"] < self.last_anchor.idx:
            o.deps.append(("op", self.last_anchor))
            self.waited_op["vector"]["vector"] = self.last_anchor.idx
        self.last_anchor = o
        for e in ENGINES:
            if e == "vector" or not self.ops[e]:
                continue
            d = self.ops[e][-1]
            if d.dma_ev is None and d.fn is not None:
                if self.waited_op["vector"][e] < d.idx:
                    d.marked = True
                    o.deps.append(("op", d))
                    self.waited_op["vector"][e] = d.idx
        for b in self.dma_bufs:
            val = 16 * b.dma_cnt
            if val > 0 and self.waited_sem["vector"].get(id(b.dma_sem), -1) < val:
                o.deps.append(("sem", b.dma_sem, val))
                self.waited_sem["vector"][id(b.dma_sem)] = val
        o.marked = True
        self.ops["vector"].append(o)
        ev = ("op", o)
        for e in ENGINES:
            if e == "vector":
                continue
            w = Op(e, None, len(self.ops[e]))
            w.deps.append(ev)
            self.ops[e].append(w)
            for f in ENGINES:
                self.waited_op[e][f] = max(self.waited_op[e][f],
                                           self.waited_op["vector"][f])
            self.waited_op[e]["vector"] = o.idx
            for k, v in self.waited_sem["vector"].items():
                if self.waited_sem[e].get(k, -1) < v:
                    self.waited_sem[e][k] = v

    def finalize(self):
        for e in ENGINES:
            c = 0
            for o in self.ops[e]:
                if o.marked and o.dma_ev is None:
                    c += 1
                    o.cnt = c
            assert c < 60000, (e, c)

    def replay(self, ename, E):
        sems = self.sems
        for o in self.ops[ename]:
            for ev in o.deps:
                if ev[0] == "op":
                    d = ev[1]
                    E.wait_ge(sems[d.eng], d.cnt)
                else:
                    E.wait_ge(ev[1], ev[2])
            if o.fn is not None:
                ins = o.fn(E)
                if o.dma_ev is not None:
                    ins.then_inc(o.dma_ev[1], 16)
                elif o.marked:
                    ins.then_inc(sems[ename], 1)


def build_host_constants(conv_k, conv_v):
    t = np.arange(S, dtype=np.float64)
    inv_freq = 1.0 / (1e5 ** (np.arange(0, HD, 2, dtype=np.float64) / HD))
    freqs = np.outer(t, inv_freq)
    cos = np.cos(freqs).T.astype(np.float32)
    sin = np.sin(freqs).T.astype(np.float32)
    c128 = np.vstack([cos, cos])
    s128 = np.vstack([-sin, sin])
    ident = np.eye(128, dtype=ml_dtypes.bfloat16)
    perm64 = np.roll(np.eye(128, dtype=ml_dtypes.bfloat16), 64, axis=1)
    qq = np.arange(128)[:, None]
    jj = np.arange(128)[None, :]
    mdiag = np.where(jj > qq, np.float32(-1e30), np.float32(0.0)).astype(ml_dtypes.bfloat16)
    mleft = np.where(jj <= qq, np.float32(-1e30), np.float32(0.0)).astype(ml_dtypes.bfloat16)
    conv_tab = np.zeros((128, 32), np.float32)
    for h in range(NKV):
        conv_tab[:, 2 * h + 0] = conv_k[h, 0]
        conv_tab[:, 2 * h + 1] = conv_k[h, 1]
        conv_tab[:, 16 + 2 * h + 0] = conv_v[h, 0]
        conv_tab[:, 16 + 2 * h + 1] = conv_v[h, 1]
    conv_tab_bf = conv_tab.astype(ml_dtypes.bfloat16)
    cs_tab = np.zeros((128, 4), np.float32)
    cs_tab[:, 0] = -0.5
    cs_tab[:, 1] = 0.5
    ones_col = np.ones((128, 1), np.float32)
    ones_row = np.ones((1, 128), np.float32)
    return dict(rope_c=c128, rope_s=s128, ident=ident, perm64=perm64,
                mdiag=mdiag, mleft=mleft, conv_f=conv_tab, conv_b=conv_tab_bf,
                cs_tab=cs_tab, ones_col=ones_col, ones_row=ones_row)


def build_program(debug_taps=False, lim_seqs=SEQ, lim_blk=NBLK, do_c=True, do_gq=True, a_stop=99, dis_rope=False, dis_conv=False, rope_mode=4, lim_c=2, no_xb=False, b_old=False):
    nc = bass.Bass("TRN2", target_bir_lowering=False)
    es = ExitStack()
    P = Prog(nc, es)

    xin = nc.dram_tensor("xin", [SEQ, S, H], F32, kind="ExternalInput").ap()
    wpack = nc.dram_tensor("wpack", [H, 4096], F32, kind="ExternalInput").ap()
    wo = nc.dram_tensor("wo", [H, H], F32, kind="ExternalInput").ap()
    w1 = nc.dram_tensor("w1", [DH, H], F32, kind="ExternalInput").ap()
    w2 = nc.dram_tensor("w2", [DH, H], F32, kind="ExternalInput").ap()
    w3 = nc.dram_tensor("w3", [H, DH], F32, kind="ExternalInput").ap()
    rope_c = nc.dram_tensor("rope_c", [128, S], F32, kind="ExternalInput").ap()
    rope_s = nc.dram_tensor("rope_s", [128, S], F32, kind="ExternalInput").ap()
    identt = nc.dram_tensor("ident", [128, 128], BF16, kind="ExternalInput").ap()
    perm64t = nc.dram_tensor("perm64", [128, 128], BF16, kind="ExternalInput").ap()
    mdiag = nc.dram_tensor("mdiag", [128, 128], BF16, kind="ExternalInput").ap()
    mleft = nc.dram_tensor("mleft", [128, 128], BF16, kind="ExternalInput").ap()
    conv_f = nc.dram_tensor("conv_f", [128, 32], F32, kind="ExternalInput").ap()
    conv_b = nc.dram_tensor("conv_b", [128, 32], BF16, kind="ExternalInput").ap()
    cs_tab = nc.dram_tensor("cs_tab", [128, 4], F32, kind="ExternalInput").ap()
    ones_col = nc.dram_tensor("ones_col", [128, 1], F32, kind="ExternalInput").ap()
    ones_row = nc.dram_tensor("ones_row", [1, 128], F32, kind="ExternalInput").ap()

    yout = nc.dram_tensor("yout", [SEQ, S, H], F32, kind="ExternalOutput").ap()

    skind = "ExternalOutput" if debug_taps else "Internal"
    qscr = nc.dram_tensor("qscr", [SEQ, NH, HD, S], BF16, kind=skind).ap()
    yscr = nc.dram_tensor("yscr", [SEQ * S, H], BF16, kind=skind).ap()
    w1sw = nc.dram_tensor("w1sw", [NKD, 128, NJ, 128], BF16, kind=skind).ap()
    w2sw = nc.dram_tensor("w2sw", [NKD, 128, NJ, 128], BF16, kind=skind).ap()
    w3sw = nc.dram_tensor("w3sw", [NKD, 128, NJ, 128], BF16, kind=skind).ap()
    if debug_taps:
        kdbg = nc.dram_tensor("kdbg", [SEQ, 128, NKV * S], BF16, kind="ExternalOutput").ap()
        vdbg = nc.dram_tensor("vdbg", [SEQ, 128, NKV * 16 * 129], BF16, kind="ExternalOutput").ap()

    def sb(name, shape, dt):
        t = es.enter_context(nc.sbuf_tensor(name, shape, dt))
        return t, Buf(name)

    ropec_sb, ropec_b = sb("ropec", [128, S], F32)
    ropes_sb, ropes_b = sb("ropes", [128, S], F32)
    ident_sb, ident_b = sb("identt", [128, 128], BF16)
    perm_sb, perm_b = sb("permt", [128, 128], BF16)
    mdiag_sb, mdiag_b = sb("mdiagt", [128, 128], BF16)
    mleft_sb, mleft_b = sb("mleftt", [128, 128], BF16)
    convf_sb, convf_b = sb("convf", [128, 32], F32)
    cst_sb, cst_b = sb("cst", [128, 4], F32)
    onesc_sb, onesc_b = sb("onesc", [128, 1], F32)
    onesr_sb, onesr_b = sb("onesr", [1, 128], F32)
    gcol_sb, gcol_b = sb("gcol", [128, 268], F32)
    gtr_sb, gtr_b = sb("gtrash", [128, 1024], BF16)
    grec_sb, grec_b = sb("grec", [128, 4], F32)
    anchor_sb, anchor_b = sb("anchor", [1, 1], F32)
    stat_sb, stat_b = sb("stat", [128, 8], F32)

    pool_t = [None, None]
    pool_t_b = [None, None]
    pool_n = [0]

    def bind_pool(scope_es):
        pool_n[0] += 1
        for i in range(2):
            pool_t[i] = scope_es.enter_context(
                nc.psum_tensor(f"pool{pool_n[0]}_{i}", [128, 512], F32))
            pool_t_b[i] = Buf(f"pool{pool_n[0]}_{i}", psum=True)

    for ap_sb, b, src in ((ropec_sb, ropec_b, rope_c), (ropes_sb, ropes_b, rope_s),
                          (ident_sb, ident_b, identt), (perm_sb, perm_b, perm64t),
                          (mdiag_sb, mdiag_b, mdiag),
                          (mleft_sb, mleft_b, mleft), (convf_sb, convf_b, conv_f),
                          (cst_sb, cst_b, cs_tab),
                          (onesc_sb, onesc_b, ones_col), (onesr_sb, onesr_b, ones_row)):
        P.dma("sync", ap_sb[:], src[:], b, writes=(b,))
    P.dma("sync", anchor_sb[:], cs_tab[0:1, 0:1], anchor_b, writes=(anchor_b,))

    def emit_tr_batch(par, items):
        n = len(items)
        for idx, (src_ap, src_b) in enumerate(items):
            P.op("tensor",
                 lambda E, pt=pool_t[par], x=idx, a=src_ap, st=(idx == 0),
                        sp=(idx == n - 1): E.matmul(
                     pt[:, x * 128:(x + 1) * 128], a, ident_sb[:],
                     start=st, stop=sp, skip_group_check=True),
                 reads=(src_b, ident_b), writes=(pool_t_b[par],))

    gamma_state = {"idx": 0}
    GAMMA_PIECES = []
    for wi, wap in ((0, w1), (1, w2)):
        for j in range(NJ):
            for hf in range(2):
                GAMMA_PIECES.append((wi, wap[j * 128:(j + 1) * 128,
                                             hf * 1024:(hf + 1) * 1024]))
    w3flat = w3.rearrange("a b -> (a b)").rearrange("(t p q) -> t p q", p=128, q=1024)
    for tsl in range(2 * NJ):
        GAMMA_PIECES.append((2, w3flat[tsl]))

    def make_gamma_stepper(wq2):
        def step(n):
            for _ in range(n):
                i = gamma_state["idx"]
                if i >= len(GAMMA_PIECES):
                    return
                gamma_state["idx"] += 1
                wi, src = GAMMA_PIECES[i]
                wq_sb, wq_b = wq2[i % 3]
                P.dma("sync" if i % 2 == 0 else "gpsimd", wq_sb[:], src, wq_b,
                      writes=(wq_b,))
                if i % 2 == 0:
                    P.op("scalar",
                         lambda E, s_=wq_sb, col=i: E.activation(
                             gtr_sb[:], s_[:], AF.Abs,
                             accum_out=gcol_sb[:, col:col + 1]),
                         reads=(wq_b,), writes=(gtr_b, gcol_b))
                else:
                    P.op("vector",
                         lambda E, s_=wq_sb, col=i: E.reduce_sum(
                             gcol_sb[:, col:col + 1], s_[:], axis=AX.X,
                             apply_absolute_value=True),
                         reads=(wq_b,), writes=(gcol_b,))
        return step

    def gamma_finalize():
        ps_g_cm = nc.psum_tensor("ps_g", [128, 4], F32)
        ps_g = ps_g_cm.__enter__()
        ps_g_b = Buf("ps_g", psum=True)
        for wi in range(3):
            P.op("vector",
                 lambda E, a=wi * 2 * NJ, bnd=(wi + 1) * 2 * NJ, col=wi: E.reduce_sum(
                     stat_sb[:, col:col + 1], gcol_sb[:, a:bnd], axis=AX.X),
                 reads=(gcol_b,), writes=(stat_b,))
        P.op("tensor",
             lambda E: E.matmul(ps_g[0:1, 0:3], onesc_sb[:], stat_sb[:, 0:3],
                                start=True, stop=True, skip_group_check=True),
             reads=(onesc_b, stat_b), writes=(ps_g_b,))
        P.op("vector",
             lambda E: E.tensor_scalar(stat_sb[0:1, 4:7], ps_g[0:1, 0:3],
                                       1.0 / NW_ELEMS, 1e-8, ALU.mult, ALU.add),
             reads=(ps_g_b,), writes=(stat_b,))
        P.op("vector",
             lambda E: E.reciprocal(stat_sb[0:1, 0:3], stat_sb[0:1, 4:7]),
             reads=(stat_b,), writes=(stat_b,))
        for wi in range(3):
            P.op("tensor",
                 lambda E, col=wi, st=(wi == 0), sp=(wi == 2): E.matmul(
                     ps_g[:, col:col + 1], onesr_sb[:],
                     stat_sb[0:1, col:col + 1],
                     start=st, stop=sp, skip_group_check=True),
                 reads=(onesr_b, stat_b), writes=(ps_g_b,))
        P.op("vector",
             lambda E: E.tensor_copy(grec_sb[:, 0:3], ps_g[:, 0:3]),
             reads=(ps_g_b,), writes=(grec_b,))
        ps_g_cm.__exit__(None, None, None)

    QUANT_PIECES = []
    for wi, wap, swap in ((0, w1, w1sw), (1, w2, w2sw)):
        for j in range(NJ):
            for qtr in range(4):
                QUANT_PIECES.append(("w12", wi, wap, swap, j, qtr))
    for c in range(NKD):
        for jg in range((NJ + 3) // 4):
            j0 = jg * 4
            jn = min(4, NJ - j0)
            QUANT_PIECES.append(("w3", 2, w3, w3sw, c, (j0, jn)))
    quant_state = {"idx": 0, "par": 0, "pend": []}

    def make_quant_stepper(slots):
        def emit_back(i):
            (wq_sb, wq_b, s1_sb, s1_b, s2_sb, s2_b, qs_sb, qs_b) = slots[i % len(slots)]
            piece = QUANT_PIECES[i]
            kind, wi, wap, swap = piece[0], piece[1], piece[2], piece[3]
            if kind == "w12":
                j, qtr = piece[4], piece[5]
                ncols = 512
            else:
                c3, (j0, jn) = piece[4], piece[5]
                ncols = jn * 128
            nb = ncols // 128
            for b0 in range(0, nb, 4):
                bn = min(4, nb - b0)
                par = quant_state["par"]
                quant_state["par"] ^= 1
                emit_tr_batch(par, [(s1_sb[:, (b0 + q) * 128:(b0 + q + 1) * 128], s1_b)
                                    for q in range(bn)])
                P.op("vector",
                     lambda E, pt=pool_t[par], a=b0, m=bn, qs=qs_sb: E.tensor_copy(
                         qs[:, a * 128:(a + m) * 128],
                         pt[:, 0:m * 128]),
                     reads=(pool_t_b[par],), writes=(qs_b,))
            if kind == "w12":
                k0 = qtr * 4
                out_ap = swap[k0:k0 + 4, :, j, :].rearrange("k p h -> p k h")
                in_ap = qs_sb[:, 0:512].rearrange("p (k h) -> p k h", k=4)
            else:
                out_ap = swap[c3, :, j0:j0 + jn, :]
                in_ap = qs_sb[:, 0:ncols].rearrange("p (j d) -> p j d", j=jn)
            P.dma("scalar", out_ap, in_ap, qs_b, reads=(qs_b,))

        def step(n):
            for _ in range(n):
                i = quant_state["idx"]
                if i >= len(QUANT_PIECES) + 2:
                    return
                if i >= 2 and quant_state["pend"]:
                    emit_back(quant_state["pend"].pop(0))
                if i >= len(QUANT_PIECES):
                    quant_state["idx"] += 1
                    continue
                quant_state["idx"] += 1
                quant_state["pend"].append(i)
                (wq_sb, wq_b, s1_sb, s1_b, s2_sb, s2_b, qs_sb, qs_b) = slots[i % len(slots)]
                piece = QUANT_PIECES[i]
                kind, wi, wap, swap = piece[0], piece[1], piece[2], piece[3]
                if kind == "w12":
                    j, qtr = piece[4], piece[5]
                    src = wap[j * 128:(j + 1) * 128, qtr * 512:(qtr + 1) * 512]
                    ncols = 512
                else:
                    c3, (j0, jn) = piece[4], piece[5]
                    src = wap[c3 * 128:(c3 + 1) * 128, j0 * 128:(j0 + jn) * 128]
                    ncols = jn * 128
                P.dma("sync" if i % 2 == 0 else "gpsimd", wq_sb[:, 0:ncols],
                      src, wq_b, writes=(wq_b,))
                P.op("scalar",
                     lambda E, n_=ncols, w_=wi, s1=s1_sb, wq=wq_sb: E.activation(
                         s1[:, 0:n_], wq[:, 0:n_], AF.Sign,
                         scale=grec_sb[:, w_:w_ + 1], bias=cst_sb[:, 0:1]),
                     reads=(wq_b, grec_b, cst_b), writes=(s1_b,))
                P.op("scalar",
                     lambda E, n_=ncols, w_=wi, s2=s2_sb, wq=wq_sb: E.activation(
                         s2[:, 0:n_], wq[:, 0:n_], AF.Sign,
                         scale=grec_sb[:, w_:w_ + 1], bias=cst_sb[:, 1:2]),
                     reads=(wq_b, grec_b, cst_b), writes=(s2_b,))
                P.op("vector",
                     lambda E, n_=ncols, s1=s1_sb, s2=s2_sb: E.tensor_tensor(
                         s1[:, 0:n_], s1[:, 0:n_], s2[:, 0:n_], ALU.add),
                     reads=(s1_b, s2_b), writes=(s1_b,))
        return step

    def emit_rope(ps_ap, ps_b, half, rope_ctx, dst_sb, dst_b):
        (rin_sb, rin_b, t1_sb, t1_b, ps_rot, ps_rot_b) = rope_ctx
        t0 = half * 1024
        P.op("scalar",
             lambda E, pa=ps_ap: E.copy(rin_sb[:, 0:1024], pa),
             reads=(ps_b,), writes=(rin_b,))
        if rope_mode >= 1:
            for jz in range(2):
                P.op("tensor",
                     lambda E, j=jz: E.matmul(
                         ps_rot[:, j * 512:(j + 1) * 512], perm_sb[:],
                         rin_sb[:, j * 512:(j + 1) * 512],
                         start=True, stop=True, skip_group_check=True),
                     reads=(rin_b, perm_b), writes=(ps_rot_b,))
        if rope_mode >= 2:
            P.op("vector",
                 lambda E, t=t0, pa=ps_ap: E.tensor_tensor(
                     t1_sb[:, 0:1024], pa, ropec_sb[:, t:t + 1024], ALU.mult),
                 reads=(ps_b, ropec_b), writes=(t1_b,))
        if rope_mode >= 3:
            P.op("vector",
                 lambda E, t=t0: E.tensor_tensor(
                     dst_sb[:, t:t + 1024], ps_rot[:], ropes_sb[:, t:t + 1024],
                     ALU.mult),
                 reads=(ps_rot_b, ropes_b), writes=(dst_b,))
        if rope_mode >= 4:
            P.op("vector",
                 lambda E, t=t0: E.tensor_tensor(
                     dst_sb[:, t:t + 1024], dst_sb[:, t:t + 1024],
                     t1_sb[:, 0:1024], ALU.add),
                 reads=(t1_b, dst_b), writes=(dst_b,))
        if rope_mode < 3:
            P.op("vector",
                 lambda E, t=t0, pa=ps_ap: E.tensor_tensor(
                     dst_sb[:, t:t + 1024], pa, ropec_sb[:, t:t + 1024], ALU.mult),
                 reads=(ps_b, ropec_b), writes=(dst_b,))

    def emit_ln_tm(src_ap, src_b, out_ap, out_b, scol):
        P.op("vector",
             lambda E, a=src_ap: E.reduce_sum(stat_sb[:, scol:scol + 1], a,
                                              axis=AX.X),
             reads=(src_b,), writes=(stat_b,))
        P.op("scalar",
             lambda E, a=src_ap, o=out_ap: E.activation(
                 o, a, AF.Square,
                 accum_out=stat_sb[:, scol + 1:scol + 2]),
             reads=(src_b,), writes=(out_b, stat_b))
        P.op("vector",
             lambda E: E.tensor_scalar(stat_sb[:, scol:scol + 1],
                                       stat_sb[:, scol:scol + 1],
                                       1.0 / H, None, ALU.mult),
             reads=(stat_b,), writes=(stat_b,))
        P.op("vector",
             lambda E: E.tensor_scalar(stat_sb[:, scol + 1:scol + 2],
                                       stat_sb[:, scol + 1:scol + 2],
                                       1.0 / H, 1e-5, ALU.mult, ALU.add),
             reads=(stat_b,), writes=(stat_b,))
        P.op("vector",
             lambda E: E.tensor_tensor(stat_sb[:, scol + 2:scol + 3],
                                       stat_sb[:, scol:scol + 1],
                                       stat_sb[:, scol:scol + 1], ALU.mult),
             reads=(stat_b,), writes=(stat_b,))
        P.op("vector",
             lambda E: E.tensor_tensor(stat_sb[:, scol + 1:scol + 2],
                                       stat_sb[:, scol + 1:scol + 2],
                                       stat_sb[:, scol + 2:scol + 3], ALU.subtract),
             reads=(stat_b,), writes=(stat_b,))
        P.op("vector",
             lambda E: E.reciprocal(stat_sb[:, scol + 2:scol + 3],
                                    stat_sb[:, scol + 1:scol + 2]),
             reads=(stat_b,), writes=(stat_b,))
        P.op("scalar",
             lambda E: E.activation(stat_sb[:, scol + 3:scol + 4],
                                    stat_sb[:, scol + 2:scol + 3], AF.Sqrt),
             reads=(stat_b,), writes=(stat_b,))
        P.op("vector",
             lambda E, a=src_ap, o=out_ap: E.tensor_scalar(
                 o, a, stat_sb[:, scol:scol + 1], stat_sb[:, scol + 3:scol + 4],
                 ALU.subtract, ALU.mult),
             reads=(src_b, stat_b), writes=(out_b,))

    quant_step = lambda n: None
    gamma_step = lambda n: None

    for s in range(lim_seqs):
        seq_es = ExitStack()
        kcache_sb = seq_es.enter_context(
            nc.sbuf_tensor(f"kcache{s}", [128, NKV * S], BF16))
        kcache_b = Buf(f"kcache{s}")
        vaug_sb = seq_es.enter_context(
            nc.sbuf_tensor(f"vaug{s}", [128, NKV * 16 * 129], BF16))
        vaug_b = Buf(f"vaug{s}")
        vaug_v = vaug_sb[:].rearrange("p (sl c) -> p sl c", c=129)
        if a_stop >= 1:
            P.op("vector",
                 lambda E, v=vaug_v: E.memset(v[:, :, 128:129], 1.0),
                 writes=(vaug_b,))

        if s == 0 and do_gq:
            gq2 = [(seq_es.enter_context(nc.sbuf_tensor(f"gwq{i}", [128, 1024], F32)),
                    Buf(f"gwq{i}")) for i in range(3)]
            gamma_step = make_gamma_stepper(gq2)
            quant_step = lambda n: None
        elif do_gq:
            qwq3 = [(seq_es.enter_context(nc.sbuf_tensor(f"qwq{i}", [128, 512], F32)),
                     Buf(f"qwq{i}")) for i in range(3)]
            qs13 = [(seq_es.enter_context(nc.sbuf_tensor(f"qs1_{i}", [128, 512], BF16)),
                     Buf(f"qs1_{i}")) for i in range(3)]
            qs2_sb = seq_es.enter_context(nc.sbuf_tensor("qs2s", [128, 512], BF16))
            qs2_bf = Buf("qs2s")
            qst3 = [(seq_es.enter_context(nc.sbuf_tensor(f"qst{i}", [128, 512], BF16)),
                     Buf(f"qst{i}")) for i in range(2)]
            slots = []
            for i in range(6):
                slots.append((qwq3[i % 3][0], qwq3[i % 3][1],
                              qs13[i % 3][0], qs13[i % 3][1],
                              qs2_sb, qs2_bf,
                              qst3[i % 2][0], qst3[i % 2][1]))
            quant_step = make_quant_stepper(slots)
            gamma_step = lambda n: None

        a_es = ExitStack()
        xst_sb, xst_b = [], []
        for i in range(2):
            t = a_es.enter_context(nc.sbuf_tensor(f"xst{s}_{i}", [128, 2048], BF16))
            xst_sb.append(t)
            xst_b.append(Buf(f"xst{s}_{i}"))
        xfm_sb = a_es.enter_context(nc.sbuf_tensor(f"xfm{s}", [128, NKD * 2048], BF16))
        xfm_b = Buf(f"xfm{s}")
        wp_sb, wp_b = [], []
        for i in range(2):
            t = a_es.enter_context(nc.sbuf_tensor(f"wp{s}_{i}", [128, 2048], BF16))
            wp_sb.append(t)
            wp_b.append(Buf(f"wp{s}_{i}"))
        wpf_sb = a_es.enter_context(nc.sbuf_tensor(f"wpf{s}", [128, 2048], F32))
        wpf_b = Buf(f"wpf{s}")
        rin_sb = xst_sb[0][:, 0:1024]
        rin_b = xst_b[0]
        t1_sb = xst_sb[1][:, 0:1024]
        t1_b = xst_b[1]
        qro0 = a_es.enter_context(nc.sbuf_tensor(f"qro{s}_0", [128, 2048], BF16))
        qro_b0 = Buf(f"qro{s}_0")
        kst_sb = a_es.enter_context(nc.sbuf_tensor(f"kst{s}", [128, 2048], BF16))
        kst_b = Buf(f"kst{s}")
        qro_sb2 = [qro0, kst_sb]
        qro_b2 = [qro_b0, kst_b]
        ctmp_sb = a_es.enter_context(nc.sbuf_tensor(f"ctmp{s}", [128, 2048], BF16))
        ctmp_b = Buf(f"ctmp{s}")
        vst_sb = a_es.enter_context(nc.sbuf_tensor(f"vst{s}", [128, 2048], BF16))
        vst_b = Buf(f"vst{s}")
        bind_pool(a_es)

        for btg in range(0 if (a_stop < 2 or no_xb) else 8):
            for b2 in range(2):
                blk = btg * 2 + b2
                P.dma("gpsimd", xst_sb[b2][:],
                      xin[s, blk * 128:(blk + 1) * 128, :], xst_b[b2],
                      writes=(xst_b[b2],))
            for k in range(NKD):
                par = k % 2
                emit_tr_batch(par, [(xst_sb[b2][:, k * 128:(k + 1) * 128], xst_b[b2])
                                    for b2 in range(2)])
                P.op("scalar",
                     lambda E, pt=pool_t[par], k_=k, g=btg, xf=xfm_sb: E.copy(
                         xf[:, k_ * 2048 + g * 256: k_ * 2048 + (g + 1) * 256],
                         pt[:, 0:256]),
                     reads=(pool_t_b[par],), writes=(xfm_b,))
            gamma_step(1)
            quant_step(1)

        qkv_es = ExitStack()
        ps_qkv = [qkv_es.enter_context(
            nc.psum_tensor(f"ps_qkv{s}_{i}", [128, 1024], F32)) for i in range(2)]
        ps_qkv_b = [Buf(f"ps_qkv{s}_{i}", psum=True) for i in range(2)]
        ps_rot = qkv_es.enter_context(nc.psum_tensor(f"ps_rot{s}", [128, 1024], F32))
        ps_rot_b = Buf(f"ps_rot{s}", psum=True)
        rope_ctx = (rin_sb, rin_b, t1_sb, t1_b, ps_rot, ps_rot_b)

        c_order = list(range(16, 24)) + list(range(24, 32)) + list(range(0, 16))
        if a_stop < 3:
            c_order = []
        elif a_stop == 3:
            c_order = c_order[:lim_c]
        wp_r = wpack.rearrange("(k p) c -> p k c", p=128)
        def emit_qkv_evac(c, half, par):
            if c < 16:
                qro_sb, qro_b = qro_sb2[c % 2], qro_b2[c % 2]
                if dis_rope:
                    P.op("scalar",
                         lambda E, pp=ps_qkv[par], h_=half, d_=qro_sb: E.copy(
                             d_[:, h_ * 1024:(h_ + 1) * 1024], pp[:, 0:1024]),
                         reads=(ps_qkv_b[par],), writes=(qro_b,))
                else:
                    emit_rope(ps_qkv[par][:, 0:1024], ps_qkv_b[par], half,
                              rope_ctx, qro_sb, qro_b)
            elif c < 24:
                if dis_rope:
                    P.op("scalar",
                         lambda E, pp=ps_qkv[par], h_=half, d_=kst_sb: E.copy(
                             d_[:, h_ * 1024:(h_ + 1) * 1024], pp[:, 0:1024]),
                         reads=(ps_qkv_b[par],), writes=(kst_b,))
                else:
                    emit_rope(ps_qkv[par][:, 0:1024], ps_qkv_b[par], half,
                              rope_ctx, kst_sb, kst_b)
            else:
                P.op("scalar",
                     lambda E, pp=ps_qkv[par], h_=half, d_=vst_sb: E.copy(
                         d_[:, h_ * 1024:(h_ + 1) * 1024], pp[:, 0:1024]),
                     reads=(ps_qkv_b[par],), writes=(vst_b,))
            if half == 1:
                emit_qkv_tail(c)

        def emit_qkv_tail(c):
            if c < 16:
                P.dma("scalar", qscr[s, c], qro_sb2[c % 2][:], qro_b2[c % 2],
                      reads=(qro_b2[c % 2],))
            elif c < 24:
                h = c - 16
                kc = kcache_sb[:, h * S:(h + 1) * S]
                if dis_conv:
                    P.op("vector",
                         lambda E, kc_=kc, ks=kst_sb: E.tensor_copy(kc_, ks[:]),
                         reads=(kst_b,), writes=(kcache_b,))
                elif True:
                    pass
                if not dis_conv:
                    P.op("vector",
                     lambda E, kc_=kc, h_=h, ks=kst_sb: E.tensor_scalar(
                         kc_, ks[:], convf_sb[:, 2 * h_ + 1:2 * h_ + 2],
                         None, ALU.mult),
                     reads=(kst_b, convf_b), writes=(kcache_b,))
                P.op("vector",
                     lambda E, h_=h, ks=kst_sb, ct=ctmp_sb: E.tensor_scalar(
                         ct[:, 0:S - 1], ks[:, 0:S - 1],
                         convf_sb[:, 2 * h_:2 * h_ + 1], None, ALU.mult),
                     reads=(kst_b, convf_b), writes=(ctmp_b,))
                P.op("vector",
                     lambda E, kc_=kc, ct=ctmp_sb: E.tensor_tensor(
                         kc_[:, 1:S], kc_[:, 1:S], ct[:, 0:S - 1], ALU.add),
                     reads=(ctmp_b, kcache_b), writes=(kcache_b,))
            else:
                h = c - 24
                sl0 = h * 16
                P.op("vector",
                     lambda E, h_=h, a=sl0, vv=vaug_v, vs=vst_sb: E.tensor_scalar(
                         vv[:, a:a + 16, 0:128],
                         vs[:].rearrange("p (sl c) -> p sl c", c=128),
                         convf_sb[:, 16 + 2 * h_ + 1:16 + 2 * h_ + 2], None,
                         ALU.mult),
                     reads=(vst_b, convf_b), writes=(vaug_b,))
                P.op("vector",
                     lambda E, h_=h, ct=ctmp_sb, vs=vst_sb: E.tensor_scalar(
                         ct[:], vs[:],
                         convf_sb[:, 16 + 2 * h_:16 + 2 * h_ + 1], None, ALU.mult),
                     reads=(vst_b, convf_b), writes=(ctmp_b,))
                P.op("vector",
                     lambda E, a=sl0, vv=vaug_v, ct=ctmp_sb: E.tensor_tensor(
                         vv[:, a:a + 16, 1:128], vv[:, a:a + 16, 1:128],
                         ct[:].rearrange("p (sl c) -> p sl c", c=128)[:, :, 0:127],
                         ALU.add),
                     reads=(ctmp_b, vaug_b), writes=(vaug_b,))
                P.op("vector",
                     lambda E, a=sl0, vv=vaug_v, ct=ctmp_sb: E.tensor_tensor(
                         vv[:, a + 1:a + 16, 0:1], vv[:, a + 1:a + 16, 0:1],
                         ct[:, 127:2047].rearrange(
                             "p (sl c) -> p sl c", c=128)[:, :, 0:1],
                         ALU.add),
                     reads=(ctmp_b, vaug_b), writes=(vaug_b,))

        pending_evac = [None]
        for ci, c in enumerate(c_order):
            wpar = ci % 2
            P.dma("sync",
                  wpf_sb[:].rearrange("p (k c) -> p k c", k=NKD),
                  wp_r[:, :, c * 128:(c + 1) * 128], wpf_b,
                  writes=(wpf_b,))
            P.op("scalar",
                 lambda E, d_=wp_sb[wpar], sr=wpf_sb: E.copy(d_[:], sr[:]),
                 reads=(wpf_b,), writes=(wp_b[wpar],))
            for half in range(2):
                par = half
                for k in range(NKD):
                    for js in range(2):
                        P.op("tensor",
                             lambda E, pp=ps_qkv[par], wpp=wp_sb[wpar], k_=k,
                                    h_=half, j_=js, st=(k == 0), xf=xfm_sb,
                                    sp=(k == NKD - 1): E.matmul(
                                 pp[:, j_ * 512:(j_ + 1) * 512],
                                 wpp[:, k_ * 128:(k_ + 1) * 128],
                                 xf[:, k_ * 2048 + h_ * 1024 + j_ * 512:
                                    k_ * 2048 + h_ * 1024 + (j_ + 1) * 512],
                                 start=st, stop=sp, skip_group_check=True),
                             reads=(wp_b[wpar], xfm_b), writes=(ps_qkv_b[par],))
                if pending_evac[0] is not None:
                    pc, ph, ppar = pending_evac[0]
                    emit_qkv_evac(pc, ph, ppar)
                pending_evac[0] = (c, half, par)
            gamma_step(7)
            quant_step(16)
        if pending_evac[0] is not None:
            pc, ph, ppar = pending_evac[0]
            emit_qkv_evac(pc, ph, ppar)
        qkv_es.close()

        for g4 in range(32 if a_stop >= 5 else 0):
            par = g4 % 2
            emit_tr_batch(par, [
                (vaug_v[:, g4 * 4 + q:g4 * 4 + q + 1, 0:128]
                 .rearrange("p a c -> p (a c)"), vaug_b) for q in range(4)])
            P.op("vector",
                 lambda E, pt=pool_t[par], a=g4 * 4, vv=vaug_v: E.tensor_copy(
                     vv[:, a:a + 4, 0:128],
                     pt[:].rearrange("p (a c) -> p a c", c=128)),
                 reads=(pool_t_b[par],), writes=(vaug_b,))
        if s == SEQ - 1 and do_gq:
            quant_step(len(QUANT_PIECES) + 4)
            assert quant_state["idx"] >= len(QUANT_PIECES)
            assert not quant_state["pend"], quant_state["pend"]
        if debug_taps:
            P.dma("gpsimd", kdbg[s], kcache_sb[:], kcache_b, reads=(kcache_b,))
            P.dma("gpsimd", vdbg[s], vaug_sb[:], vaug_b, reads=(vaug_b,))
        a_es.close()
        P.barrier(anchor_sb[:])

        if s == 0 and do_gq:
            gamma_step(len(GAMMA_PIECES))
            gamma_finalize()
            P.barrier(anchor_sb[:])

        b_es = ExitStack()
        wo_sb = b_es.enter_context(nc.sbuf_tensor(f"wo_sb{s}", [128, NKD * 2048], BF16))
        wo_b = Buf(f"wo_sb{s}")
        qsb_sb, qsb_b, xr_sb, xr_b = [], [], [], []
        for i in range(2):
            t = b_es.enter_context(nc.sbuf_tensor(f"qsb{s}_{i}", [128, 2048], BF16))
            qsb_sb.append(t)
            qsb_b.append(Buf(f"qsb{s}_{i}"))
            t = b_es.enter_context(nc.sbuf_tensor(f"xr{s}_{i}", [128, 2048], F32))
            xr_sb.append(t)
            xr_b.append(Buf(f"xr{s}_{i}"))
        pT_sb, pT_b = [], []
        for i in range(6):
            t = b_es.enter_context(nc.sbuf_tensor(f"pT{s}_{i}", [128, 512], BF16))
            pT_sb.append(t)
            pT_b.append(Buf(f"pT{s}_{i}"))
        atm_sb, atm_b = [], []
        for i in range(3):
            t = b_es.enter_context(nc.sbuf_tensor(f"atm{s}_{i}", [128, 128], BF16))
            atm_sb.append(t)
            atm_b.append(Buf(f"atm{s}_{i}"))
        afm_sb = b_es.enter_context(nc.sbuf_tensor(f"afm{s}", [128, 2048], BF16))
        afm_b = Buf(f"afm{s}")
        ysb_sb = b_es.enter_context(nc.sbuf_tensor(f"ysb{s}", [128, 2048], F32))
        ysb_b = Buf(f"ysb{s}")
        yn_sb = b_es.enter_context(nc.sbuf_tensor(f"yn{s}", [128, 2048], BF16))
        yn_b = Buf(f"yn{s}")
        rcol_sb = stat_sb[:, 6:8]
        rcol_b = stat_b

        ps_b_es = ExitStack()
        NSS = 2 if b_old else 4
        ps_s = [ps_b_es.enter_context(nc.psum_tensor(f"ps_s{s}_{i}", [128, 512], F32))
                for i in range(NSS)]
        ps_s_b = [Buf(f"ps_s{s}_{i}", psum=True) for i in range(NSS)]
        if b_old:
            bind_pool(ps_b_es)
        ps_o = [ps_b_es.enter_context(nc.psum_tensor(f"ps_o{s}_{i}", [128, 129], F32))
                for i in range(2)]
        ps_o_b = [Buf(f"ps_o{s}_{i}", psum=True) for i in range(2)]
        ps_y = [ps_b_es.enter_context(nc.psum_tensor(f"ps_y{s}_{i}", [128, 512], F32))
                for i in range(2)]
        ps_y_b = [Buf(f"ps_y{s}_{i}", psum=True) for i in range(2)]

        wo_r = wo.rearrange("(k p) d -> p k d", p=128)
        P.dma("gpsimd", wo_sb[:].rearrange("p (k d) -> p k d", k=NKD),
              wo_r, wo_b, writes=(wo_b,))

        qscr_r = qscr[s].rearrange("h p t -> p h t")
        sgrp = 0
        for i in range(lim_blk):
            ipar = i % 2
            P.dma("sync",
                  qsb_sb[ipar][:].rearrange("p (h t) -> p h t", h=NH),
                  qscr_r[:, :, i * 128:(i + 1) * 128], qsb_b[ipar],
                  writes=(qsb_b[ipar],))
            P.dma("sync", xr_sb[ipar][:], xin[s, i * 128:(i + 1) * 128, :],
                  xr_b[ipar], writes=(xr_b[ipar],))
            kt_lo = max(0, i - KTW)
            kts = list(range(kt_lo, i + 1))
            head_groups = {}

            def emit_scores(h):
                nonlocal sgrp
                kv = h // 2
                glist = []
                for g0 in range(0, len(kts), 4):
                    grp = kts[g0:g0 + 4]
                    spar = sgrp % 6
                    sspar = sgrp % NSS
                    sgrp += 1
                    glist.append((spar, grp))
                    mmseq = []
                    for idx, kt in enumerate(grp):
                        mmseq.append((idx, kt, False))
                        if kt == i:
                            mmseq.append((idx, kt, "diag"))
                        elif kt == i - KTW:
                            mmseq.append((idx, kt, "left"))
                    nmm = len(mmseq)
                    for mi, (idx, kt, mk) in enumerate(mmseq):
                        st = (mi == 0)
                        sp = (mi == nmm - 1)
                        if mk is False:
                            P.op("tensor",
                                 lambda E, pp=ps_s[sspar], x=idx, kv_=kv, kt_=kt,
                                        h_=h, kcs=kcache_sb, qs=qsb_sb[ipar],
                                        st_=st, sp_=sp: E.matmul(
                                     pp[:, x * 128:(x + 1) * 128],
                                     kcs[:, kv_ * S + kt_ * 128:
                                         kv_ * S + (kt_ + 1) * 128],
                                     qs[:, h_ * 128:(h_ + 1) * 128],
                                     start=st_, stop=sp_, skip_group_check=True),
                                 reads=(kcache_b, qsb_b[ipar]),
                                 writes=(ps_s_b[sspar],))
                        else:
                            msk_sb = mdiag_sb if mk == "diag" else mleft_sb
                            msk_b = mdiag_b if mk == "diag" else mleft_b
                            P.op("tensor",
                                 lambda E, pp=ps_s[sspar], x=idx, m_=msk_sb,
                                        st_=st, sp_=sp: E.matmul(
                                     pp[:, x * 128:(x + 1) * 128],
                                     m_[:], ident_sb[:],
                                     start=st_, stop=sp_, skip_group_check=True),
                                 reads=(msk_b, ident_b), writes=(ps_s_b[sspar],))
                    n = len(grp) * 128
                    P.op("scalar",
                         lambda E, pt=pT_sb[spar], pp=ps_s[sspar], n_=n:
                         E.activation(pt[:, 0:n_], pp[:, 0:n_], AF.Exp,
                                      scale=INV_SQRT_HD),
                         reads=(ps_s_b[sspar],), writes=(pT_b[spar],))
                head_groups[h] = glist

            def emit_attnv(h):
                kv = h // 2
                opar = h % 2
                for spar, grp in head_groups[h]:
                    for idx, kt in enumerate(grp):
                        P.op("tensor",
                             lambda E, po=ps_o[opar], pt=pT_sb[spar], x=idx,
                                    kv_=kv, kt_=kt, vv=vaug_v,
                                    st=(kt == kts[0]), sp=(kt == kts[-1]): E.matmul(
                                 po[:],
                                 pt[:, x * 128:(x + 1) * 128],
                                 vv[:, kv_ * 16 + kt_:kv_ * 16 + kt_ + 1, :]
                                 .rearrange("p a c -> p (a c)"),
                                 start=st, stop=sp, skip_group_check=True),
                             reads=(pT_b[spar], vaug_b), writes=(ps_o_b[opar],))
                P.op("vector",
                     lambda E, o_=opar, rl=rcol_sb, po=ps_o[opar]: E.reciprocal(
                         rl[:, o_:o_ + 1], po[:, 128:129]),
                     reads=(ps_o_b[opar],), writes=(rcol_b,))
                tpar = h % 3
                P.op("scalar",
                     lambda E, o_=opar, at=atm_sb[tpar], po=ps_o[opar],
                            rl=rcol_sb: E.activation(
                         at[:], po[:, 0:128], AF.Copy,
                         scale=rl[:, o_:o_ + 1]),
                     reads=(ps_o_b[opar], rcol_b), writes=(atm_b[tpar],))

            def emit_attn_tr(h):
                tpar = h % 3
                opar = h % 2
                if b_old:
                    P.op("tensor",
                         lambda E, pt=pool_t[opar], at=atm_sb[tpar]: E.matmul(
                             pt[:, 0:128], at[:], ident_sb[:],
                             start=True, stop=True, skip_group_check=True),
                         reads=(atm_b[tpar], ident_b), writes=(pool_t_b[opar],))
                    P.op("scalar",
                         lambda E, pt=pool_t[opar], h_=h, af=afm_sb: E.copy(
                             af[:, h_ * 128:(h_ + 1) * 128], pt[:, 0:128]),
                         reads=(pool_t_b[opar],), writes=(afm_b,))
                    return
                P.op("tensor",
                     lambda E, py=ps_y[opar], at=atm_sb[tpar]: E.matmul(
                         py[:, 0:128], at[:], ident_sb[:],
                         start=True, stop=True, skip_group_check=True),
                     reads=(atm_b[tpar], ident_b), writes=(ps_y_b[opar],))
                P.op("vector",
                     lambda E, py=ps_y[opar], h_=h, af=afm_sb: E.tensor_copy(
                         af[:, h_ * 128:(h_ + 1) * 128], py[:, 0:128]),
                     reads=(ps_y_b[opar],), writes=(afm_b,))

            for h in range(NH):
                emit_scores(h)
                if h >= 1:
                    emit_attnv(h - 1)
                if h >= 2:
                    emit_attn_tr(h - 2)
            emit_attnv(NH - 1)
            emit_attn_tr(NH - 2)
            emit_attn_tr(NH - 1)
            for js in range(4):
                ypar = js % 2
                for cch in range(NKD):
                    P.op("tensor",
                         lambda E, py=ps_y[ypar], c_=cch, j_=js, st=(cch == 0),
                                af=afm_sb, wos=wo_sb,
                                sp=(cch == NKD - 1): E.matmul(
                             py[:], af[:, c_ * 128:(c_ + 1) * 128],
                             wos[:, c_ * 2048 + j_ * 512:
                                 c_ * 2048 + (j_ + 1) * 512],
                             start=st, stop=sp, skip_group_check=True),
                         reads=(afm_b, wo_b), writes=(ps_y_b[ypar],))
                P.op("vector",
                     lambda E, py=ps_y[ypar], j_=js, ys=ysb_sb,
                            xr=xr_sb[ipar]: E.tensor_tensor(
                         ys[:, j_ * 512:(j_ + 1) * 512], py[:],
                         xr[:, j_ * 512:(j_ + 1) * 512], ALU.add),
                     reads=(ps_y_b[ypar], xr_b[ipar]), writes=(ysb_b,))
            emit_ln_tm(ysb_sb[:], ysb_b, yn_sb[:], yn_b, 0)
            P.dma("gpsimd", yscr[(s * S + i * 128):(s * S + (i + 1) * 128), :],
                  yn_sb[:], yn_b, reads=(yn_b,))
        ps_b_es.close()
        b_es.close()
        seq_es.close()
        P.barrier(anchor_sb[:])

    if do_gq:
        assert gamma_state["idx"] >= len(GAMMA_PIECES)
    P.barrier(anchor_sb[:])

    c_es = ExitStack()
    yfm_sb = c_es.enter_context(nc.sbuf_tensor("yfm", [128, NKD * GT], BF16))
    yfm_b = Buf("yfm")
    h_sb = c_es.enter_context(nc.sbuf_tensor("hsb", [128, NJ * GT], BF16))
    h_b = Buf("hsb")
    ytm_sb, ytm_b = [], []
    for i in range(4):
        t = c_es.enter_context(nc.sbuf_tensor(f"ytm{i}", [128, 2048], BF16))
        ytm_sb.append(t)
        ytm_b.append(Buf(f"ytm{i}"))
    w1t_sb, w1t_b, w2t_sb, w2t_b = [], [], [], []
    for i in range(2):
        t = c_es.enter_context(nc.sbuf_tensor(f"w1t{i}", [128, NKD * 256], BF16))
        w1t_sb.append(t)
        w1t_b.append(Buf(f"w1t{i}"))
        t = c_es.enter_context(nc.sbuf_tensor(f"w2t{i}", [128, NKD * 256], BF16))
        w2t_sb.append(t)
        w2t_b.append(Buf(f"w2t{i}"))
    w3t_sb, w3t_b = [], []
    for i in range(2):
        t = c_es.enter_context(nc.sbuf_tensor(f"w3t{i}", [128, NJ * 128], BF16))
        w3t_sb.append(t)
        w3t_b.append(Buf(f"w3t{i}"))
    sg_sb = c_es.enter_context(nc.sbuf_tensor("sg", [128, GT], F32))
    sg_b = Buf("sg")
    ob_sb = c_es.enter_context(nc.sbuf_tensor("ob", [128, GT], BF16))
    ob_b = Buf("ob")
    fm_sb = c_es.enter_context(nc.sbuf_tensor("fm", [128, GT], BF16))
    fm_b = Buf("fm")
    tm_sb = c_es.enter_context(nc.sbuf_tensor("tm", [128, NTT * 2048], BF16))
    tm_b = Buf("tm")
    of_sb = c_es.enter_context(nc.sbuf_tensor("of", [128, 2048], F32))
    of_b = Buf("of")
    bind_pool(c_es)

    ps_c_es = ExitStack()
    ps_x1 = [ps_c_es.enter_context(nc.psum_tensor(f"ps_x1{i}", [128, GT], F32))
             for i in range(2)]
    ps_x1_b = [Buf(f"ps_x1{i}", psum=True) for i in range(2)]
    ps_x2 = [ps_c_es.enter_context(nc.psum_tensor(f"ps_x2{i}", [128, GT], F32))
             for i in range(2)]
    ps_x2_b = [Buf(f"ps_x2{i}", psum=True) for i in range(2)]
    ps_of = [ps_c_es.enter_context(nc.psum_tensor(f"ps_of{i}", [128, GT], F32))
             for i in range(2)]
    ps_of_b = [Buf(f"ps_of{i}", psum=True) for i in range(2)]

    tm_v = tm_sb[:].rearrange("p (t d) -> p t d", t=NTT)
    for g in range(NG if do_c else 0):
        gs = g // (S // GT)
        grow = g * GT
        for bt in range(NTT):
            P.dma("gpsimd", ytm_sb[bt][:],
                  yscr[grow + bt * 128: grow + (bt + 1) * 128, :],
                  ytm_b[bt], writes=(ytm_b[bt],))
        for k in range(NKD):
            par = k % 2
            emit_tr_batch(par, [(ytm_sb[bt][:, k * 128:(k + 1) * 128], ytm_b[bt])
                                for bt in range(NTT)])
            P.op("vector",
                 lambda E, pt=pool_t[par], k_=k: E.tensor_copy(
                     yfm_sb[:, k_ * GT:(k_ + 1) * GT], pt[:]),
                 reads=(pool_t_b[par],), writes=(yfm_b,))
        for jp in range(NJ // 2):
            wpar = jp % 2
            P.dma("sync",
                  w1t_sb[wpar][:].rearrange("p (k j h) -> p k j h", k=NKD, j=2),
                  w1sw[:, :, jp * 2:(jp + 1) * 2, :].rearrange("k p j h -> p k j h"),
                  w1t_b[wpar], writes=(w1t_b[wpar],))
            P.dma("scalar",
                  w2t_sb[wpar][:].rearrange("p (k j h) -> p k j h", k=NKD, j=2),
                  w2sw[:, :, jp * 2:(jp + 1) * 2, :].rearrange("k p j h -> p k j h"),
                  w2t_b[wpar], writes=(w2t_b[wpar],))
            for jj in range(2):
                j = jp * 2 + jj
                xpar = j % 2
                for k in range(NKD):
                    P.op("tensor",
                         lambda E, p=xpar, wp_=wpar, k_=k, jj_=jj, st=(k == 0),
                                sp=(k == NKD - 1): E.matmul(
                             ps_x1[p][:],
                             w1t_sb[wp_][:, (k_ * 2 + jj_) * 128:
                                          (k_ * 2 + jj_ + 1) * 128],
                             yfm_sb[:, k_ * GT:(k_ + 1) * GT],
                             start=st, stop=sp, skip_group_check=True),
                         reads=(w1t_b[wpar], yfm_b), writes=(ps_x1_b[xpar],))
                for k in range(NKD):
                    P.op("tensor",
                         lambda E, p=xpar, wp_=wpar, k_=k, jj_=jj, st=(k == 0),
                                sp=(k == NKD - 1): E.matmul(
                             ps_x2[p][:],
                             w2t_sb[wp_][:, (k_ * 2 + jj_) * 128:
                                          (k_ * 2 + jj_ + 1) * 128],
                             yfm_sb[:, k_ * GT:(k_ + 1) * GT],
                             start=st, stop=sp, skip_group_check=True),
                         reads=(w2t_b[wpar], yfm_b), writes=(ps_x2_b[xpar],))
                P.op("scalar",
                     lambda E, p=xpar: E.activation(
                         sg_sb[:], ps_x1[p][:], AF.Sigmoid, scale=0.5),
                     reads=(ps_x1_b[xpar],), writes=(sg_b,))
                P.op("vector",
                     lambda E, p=xpar, j_=j: E.tensor_tensor(
                         h_sb[:, j_ * GT:(j_ + 1) * GT], sg_sb[:], ps_x2[p][:],
                         ALU.mult),
                     reads=(sg_b, ps_x2_b[xpar]), writes=(h_b,))
        def emit_cb_evac(c):
            cpar = c % 2
            P.op("scalar",
                 lambda E, p=cpar: E.activation(
                     ob_sb[:], ps_of[p][:], AF.Copy, scale=0.25),
                 reads=(ps_of_b[cpar],), writes=(ob_b,))
            P.op("vector",
                 lambda E, c_=c: E.tensor_tensor(
                     fm_sb[:], ob_sb[:], yfm_sb[:, c_ * GT:(c_ + 1) * GT],
                     ALU.add),
                 reads=(ob_b, yfm_b), writes=(fm_b,))
            tpar = c % 2
            emit_tr_batch(tpar, [(fm_sb[:, tt * 128:(tt + 1) * 128], fm_b)
                                 for tt in range(NTT)])
            P.op("vector",
                 lambda E, pt=pool_t[tpar], c_=c: E.tensor_copy(
                     tm_v[:, :, c_ * 128:(c_ + 1) * 128],
                     pt[:].rearrange("p (t c) -> p t c", c=128)),
                 reads=(pool_t_b[tpar],), writes=(tm_b,))

        pend_cb = [None]
        for c in range(NKD):
            cpar = c % 2
            P.dma("sync", w3t_sb[cpar][:],
                  w3sw[c].rearrange("p j d -> p (j d)"),
                  w3t_b[cpar], writes=(w3t_b[cpar],))
            for j in range(NJ):
                P.op("tensor",
                     lambda E, p=cpar, j_=j, st=(j == 0), sp=(j == NJ - 1): E.matmul(
                         ps_of[p][:], w3t_sb[p][:, j_ * 128:(j_ + 1) * 128],
                         h_sb[:, j_ * GT:(j_ + 1) * GT],
                         start=st, stop=sp, skip_group_check=True),
                     reads=(w3t_b[cpar], h_b), writes=(ps_of_b[cpar],))
            if pend_cb[0] is not None:
                emit_cb_evac(pend_cb[0])
            pend_cb[0] = c
        emit_cb_evac(pend_cb[0])
        for tt in range(NTT):
            emit_ln_tm(tm_sb[:, tt * 2048:(tt + 1) * 2048], tm_b,
                       of_sb[:], of_b, 4)
            r0 = (grow + tt * 128) % S
            P.dma("gpsimd", yout[gs, r0:r0 + 128, :], of_sb[:], of_b,
                  reads=(of_b,))
    ps_c_es.close()
    c_es.close()

    P.barrier(anchor_sb[:])
    P.finalize()

    with nc.Block() as block:
        block.sync(lambda E: P.replay("sync", E))
        block.gpsimd(lambda E: P.replay("gpsimd", E))
        block.scalar(lambda E: P.replay("scalar", E))
        block.vector(lambda E: P.replay("vector", E))
        block.tensor(lambda E: P.replay("tensor", E))

    es.close()
    nc._P = P
    return nc


def kernel(L0_attn_hidden, L0_attn_W_pack, L0_attn_W_o, L0_attn_conv_k,
           L0_attn_conv_v, L0_ffn_w1, L0_ffn_w2, L0_ffn_w3):
    consts = build_host_constants(np.asarray(L0_attn_conv_k),
                                  np.asarray(L0_attn_conv_v))
    nc = build_program()
    shared = {
        "wpack": np.ascontiguousarray(np.asarray(L0_attn_W_pack)),
        "wo": np.ascontiguousarray(np.asarray(L0_attn_W_o)),
        "w1": np.ascontiguousarray(np.asarray(L0_ffn_w1)),
        "w2": np.ascontiguousarray(np.asarray(L0_ffn_w2)),
        "w3": np.ascontiguousarray(np.asarray(L0_ffn_w3)),
        "rope_c": consts["rope_c"], "rope_s": consts["rope_s"],
        "ident": consts["ident"], "perm64": consts["perm64"],
        "mdiag": consts["mdiag"],
        "mleft": consts["mleft"], "conv_f": consts["conv_f"],
        "conv_b": consts["conv_b"], "cs_tab": consts["cs_tab"],
        "ones_col": consts["ones_col"], "ones_row": consts["ones_row"],
    }
    x = np.asarray(L0_attn_hidden)
    in_maps = []
    for c in range(NCORES):
        m = dict(shared)
        m["xin"] = np.ascontiguousarray(x[c * SEQ:(c + 1) * SEQ])
        in_maps.append(m)
    res = run_bass_kernel_spmd(nc, in_maps, core_ids=list(range(NCORES)))
    out = np.concatenate([r["yout"] for r in res.results], axis=0)
    return np.asarray(out, dtype=np.float32)


# version 12 (final)
# speedup vs baseline: 1.2711x; 1.2711x over previous
import numpy as np
import ml_dtypes
from contextlib import ExitStack

import concourse.bass as bass
import concourse.mybir as mybir
from concourse.bass_utils import run_bass_kernel_spmd

F32 = mybir.dt.float32
BF16 = mybir.dt.bfloat16
AF = mybir.ActivationFunctionType
ALU = mybir.AluOpType
AX = mybir.AxisListType

NCORES = 8
SEQ = 2
S = 2048
H = 2048
NH = 16
NKV = 8
HD = 128
WIN = 1024
NBLK = 16
KTW = WIN // 128
DH = 5632
NJ = 44
NKD = 16
GT = 512
NG = (SEQ * S) // GT
NTT = GT // 128
INV_SQRT_HD = float(1.0 / np.sqrt(np.float32(HD)))
NW_ELEMS = float(DH * H)

ENGINES = ("sync", "gpsimd", "scalar", "vector", "tensor")
DMA_ENGINES = ("sync", "gpsimd", "scalar")


class Buf:
    __slots__ = ("name", "lw", "readers", "dma_sem", "dma_cnt", "psum")

    def __init__(self, name, psum=False):
        self.name = name
        self.lw = None
        self.readers = []
        self.dma_sem = None
        self.dma_cnt = 0
        self.psum = psum


class Op:
    __slots__ = ("eng", "fn", "deps", "marked", "idx", "cnt", "dma_ev")

    def __init__(self, eng, fn, idx):
        self.eng = eng
        self.fn = fn
        self.deps = []
        self.marked = False
        self.idx = idx
        self.cnt = None
        self.dma_ev = None


class Prog:
    def __init__(self, nc, es):
        self.nc = nc
        self.es = es
        self.sems = {e: es.enter_context(nc.semaphore(f"sem_{e}")) for e in ENGINES}
        self.ops = {e: [] for e in ENGINES}
        self.waited_op = {e: {f: -1 for f in ENGINES} for e in ENGINES}
        self.waited_sem = {e: {} for e in ENGINES}
        self.dma_bufs = []
        self.sem_objs = {}

    def buf_sem(self, buf):
        if buf.dma_sem is None:
            buf.dma_sem = self.es.enter_context(
                self.nc.semaphore(f"ds_{buf.name}"))
            self.dma_bufs.append(buf)
            self.sem_objs[id(buf.dma_sem)] = buf.dma_sem
        return buf.dma_sem

    def op(self, eng, fn, reads=(), writes=(), dma_buf=None):
        o = Op(eng, fn, len(self.ops[eng]))
        deps = []
        for b in reads:
            if b.lw is not None:
                deps.append(b.lw)
            if b.psum and eng in ("scalar", "vector"):
                other = "vector" if eng == "scalar" else "scalar"
                for ev in reversed(b.readers):
                    if ev[0] == "op" and ev[1].eng == other:
                        deps.append(ev)
                        break
        for b in writes:
            if b.lw is not None:
                deps.append(b.lw)
            deps.extend(b.readers)
        for ev in deps:
            if ev[0] == "op":
                d = ev[1]
                if d.dma_ev is not None:
                    sem, val = d.dma_ev[1], d.dma_ev[2]
                    if self.waited_sem[eng].get(id(sem), -1) < val:
                        o.deps.append(("sem", sem, val))
                        self.waited_sem[eng][id(sem)] = val
                else:
                    if d.eng == eng == "tensor":
                        continue
                    if self.waited_op[eng][d.eng] < d.idx:
                        d.marked = True
                        o.deps.append(("op", d))
                        self.waited_op[eng][d.eng] = d.idx
            else:
                _, sem, val = ev
                if self.waited_sem[eng].get(id(sem), -1) < val:
                    o.deps.append(ev)
                    self.waited_sem[eng][id(sem)] = val
        if dma_buf is not None:
            assert eng in DMA_ENGINES
            sem = self.buf_sem(dma_buf)
            dma_buf.dma_cnt += 1
            o.dma_ev = ("sem", sem, 16 * dma_buf.dma_cnt)
        self.ops[eng].append(o)
        ev = ("op", o)
        for b in reads:
            b.readers.append(ev)
        for b in writes:
            b.lw = ev
            b.readers = []
        return o

    def dma(self, eng, out_ap, in_ap, buf, reads=(), writes=()):
        def fn(E, o=out_ap, i=in_ap):
            return E.dma_start(out=o, in_=i)
        return self.op(eng, fn, reads=reads, writes=writes, dma_buf=buf)

    last_anchor = None
    BARRIER_MODE = "anchor"

    def barrier_direct(self):
        snap = {}
        for e in ENGINES:
            last = None
            for o in reversed(self.ops[e]):
                if o.fn is not None and o.dma_ev is None:
                    last = o
                    break
            snap[e] = last
        semsnap = [(b.dma_sem, 16 * b.dma_cnt) for b in self.dma_bufs if b.dma_cnt > 0]
        for e in ENGINES:
            w = Op(e, None, len(self.ops[e]))
            for f in ENGINES:
                d = snap[f]
                if f == e or d is None:
                    continue
                if self.waited_op[e][f] < d.idx:
                    d.marked = True
                    w.deps.append(("op", d))
                    self.waited_op[e][f] = d.idx
            for sem, val in semsnap:
                if self.waited_sem[e].get(id(sem), -1) < val:
                    w.deps.append(("sem", sem, val))
                    self.waited_sem[e][id(sem)] = val
            self.ops[e].append(w)

    def barrier(self, anchor_ap):
        if self.BARRIER_MODE == "direct":
            return self.barrier_direct()
        return self.barrier_anchor(anchor_ap)

    def barrier_anchor(self, anchor_ap):
        def anchor_fn(E, ap=anchor_ap):
            return E.tensor_scalar(ap, ap, 1.0, None, ALU.mult)
        o = Op("vector", anchor_fn, len(self.ops["vector"]))
        if self.last_anchor is not None and \
                self.waited_op["vector"]["vector"] < self.last_anchor.idx:
            o.deps.append(("op", self.last_anchor))
            self.waited_op["vector"]["vector"] = self.last_anchor.idx
        self.last_anchor = o
        for e in ENGINES:
            if e == "vector" or not self.ops[e]:
                continue
            d = self.ops[e][-1]
            if d.dma_ev is None and d.fn is not None:
                if self.waited_op["vector"][e] < d.idx:
                    d.marked = True
                    o.deps.append(("op", d))
                    self.waited_op["vector"][e] = d.idx
        for b in self.dma_bufs:
            val = 16 * b.dma_cnt
            if val > 0 and self.waited_sem["vector"].get(id(b.dma_sem), -1) < val:
                o.deps.append(("sem", b.dma_sem, val))
                self.waited_sem["vector"][id(b.dma_sem)] = val
        o.marked = True
        self.ops["vector"].append(o)
        ev = ("op", o)
        for e in ENGINES:
            if e == "vector":
                continue
            w = Op(e, None, len(self.ops[e]))
            w.deps.append(ev)
            self.ops[e].append(w)
            for f in ENGINES:
                self.waited_op[e][f] = max(self.waited_op[e][f],
                                           self.waited_op["vector"][f])
            self.waited_op[e]["vector"] = o.idx
            for k, v in self.waited_sem["vector"].items():
                if self.waited_sem[e].get(k, -1) < v:
                    self.waited_sem[e][k] = v

    def finalize(self):
        for e in ENGINES:
            c = 0
            for o in self.ops[e]:
                if o.marked and o.dma_ev is None:
                    c += 1
                    o.cnt = c
            assert c < 60000, (e, c)

    def replay(self, ename, E):
        sems = self.sems
        for o in self.ops[ename]:
            for ev in o.deps:
                if ev[0] == "op":
                    d = ev[1]
                    E.wait_ge(sems[d.eng], d.cnt)
                else:
                    E.wait_ge(ev[1], ev[2])
            if o.fn is not None:
                ins = o.fn(E)
                if o.dma_ev is not None:
                    ins.then_inc(o.dma_ev[1], 16)
                elif o.marked:
                    ins.then_inc(sems[ename], 1)


def build_host_constants(conv_k, conv_v):
    t = np.arange(S, dtype=np.float64)
    inv_freq = 1.0 / (1e5 ** (np.arange(0, HD, 2, dtype=np.float64) / HD))
    freqs = np.outer(t, inv_freq)
    cos = np.cos(freqs).T.astype(np.float32)
    sin = np.sin(freqs).T.astype(np.float32)
    c128 = np.vstack([cos, cos])
    s128 = np.vstack([-sin, sin])
    ident = np.eye(128, dtype=ml_dtypes.bfloat16)
    perm64 = np.roll(np.eye(128, dtype=ml_dtypes.bfloat16), 64, axis=1)
    qq = np.arange(128)[:, None]
    jj = np.arange(128)[None, :]
    mdiag = np.where(jj > qq, np.float32(-1e30), np.float32(0.0)).astype(ml_dtypes.bfloat16)
    mleft = np.where(jj <= qq, np.float32(-1e30), np.float32(0.0)).astype(ml_dtypes.bfloat16)
    conv_tab = np.zeros((128, 32), np.float32)
    for h in range(NKV):
        conv_tab[:, 2 * h + 0] = conv_k[h, 0]
        conv_tab[:, 2 * h + 1] = conv_k[h, 1]
        conv_tab[:, 16 + 2 * h + 0] = conv_v[h, 0]
        conv_tab[:, 16 + 2 * h + 1] = conv_v[h, 1]
    conv_tab_bf = conv_tab.astype(ml_dtypes.bfloat16)
    cs_tab = np.zeros((128, 4), np.float32)
    cs_tab[:, 0] = -0.5
    cs_tab[:, 1] = 0.5
    ones_col = np.ones((128, 1), np.float32)
    ones_row = np.ones((1, 128), np.float32)
    return dict(rope_c=c128, rope_s=s128, ident=ident, perm64=perm64,
                mdiag=mdiag, mleft=mleft, conv_f=conv_tab, conv_b=conv_tab_bf,
                cs_tab=cs_tab, ones_col=ones_col, ones_row=ones_row)


def build_program(debug_taps=False, lim_seqs=SEQ, lim_blk=NBLK, do_c=True, do_gq=True, a_stop=99, dis_rope=False, dis_conv=False, rope_mode=4, lim_c=2, no_xb=False, b_old=False):
    nc = bass.Bass("TRN2", target_bir_lowering=False)
    es = ExitStack()
    P = Prog(nc, es)

    xin = nc.dram_tensor("xin", [SEQ, S, H], F32, kind="ExternalInput").ap()
    wpack = nc.dram_tensor("wpack", [H, 4096], F32, kind="ExternalInput").ap()
    wo = nc.dram_tensor("wo", [H, H], F32, kind="ExternalInput").ap()
    w1 = nc.dram_tensor("w1", [DH, H], F32, kind="ExternalInput").ap()
    w2 = nc.dram_tensor("w2", [DH, H], F32, kind="ExternalInput").ap()
    w3 = nc.dram_tensor("w3", [H, DH], F32, kind="ExternalInput").ap()
    rope_c = nc.dram_tensor("rope_c", [128, S], F32, kind="ExternalInput").ap()
    rope_s = nc.dram_tensor("rope_s", [128, S], F32, kind="ExternalInput").ap()
    identt = nc.dram_tensor("ident", [128, 128], BF16, kind="ExternalInput").ap()
    perm64t = nc.dram_tensor("perm64", [128, 128], BF16, kind="ExternalInput").ap()
    mdiag = nc.dram_tensor("mdiag", [128, 128], BF16, kind="ExternalInput").ap()
    mleft = nc.dram_tensor("mleft", [128, 128], BF16, kind="ExternalInput").ap()
    conv_f = nc.dram_tensor("conv_f", [128, 32], F32, kind="ExternalInput").ap()
    conv_b = nc.dram_tensor("conv_b", [128, 32], BF16, kind="ExternalInput").ap()
    cs_tab = nc.dram_tensor("cs_tab", [128, 4], F32, kind="ExternalInput").ap()
    ones_col = nc.dram_tensor("ones_col", [128, 1], F32, kind="ExternalInput").ap()
    ones_row = nc.dram_tensor("ones_row", [1, 128], F32, kind="ExternalInput").ap()

    yout = nc.dram_tensor("yout", [SEQ, S, H], F32, kind="ExternalOutput").ap()

    skind = "ExternalOutput" if debug_taps else "Internal"
    qscr = nc.dram_tensor("qscr", [SEQ, NH, HD, S], BF16, kind=skind).ap()
    yscr = nc.dram_tensor("yscr", [SEQ * S, H], BF16, kind=skind).ap()
    w1sw = nc.dram_tensor("w1sw", [NKD, 128, NJ, 128], BF16, kind=skind).ap()
    w2sw = nc.dram_tensor("w2sw", [NKD, 128, NJ, 128], BF16, kind=skind).ap()
    w3sw = nc.dram_tensor("w3sw", [NKD, 128, NJ, 128], BF16, kind=skind).ap()
    if debug_taps:
        kdbg = nc.dram_tensor("kdbg", [SEQ, 128, NKV * S], BF16, kind="ExternalOutput").ap()
        vdbg = nc.dram_tensor("vdbg", [SEQ, 128, NKV * 16 * 129], BF16, kind="ExternalOutput").ap()

    def sb(name, shape, dt):
        t = es.enter_context(nc.sbuf_tensor(name, shape, dt))
        return t, Buf(name)

    ropec_sb, ropec_b = sb("ropec", [128, S], F32)
    ropes_sb, ropes_b = sb("ropes", [128, S], F32)
    ident_sb, ident_b = sb("identt", [128, 128], BF16)
    perm_sb, perm_b = sb("permt", [128, 128], BF16)
    mdiag_sb, mdiag_b = sb("mdiagt", [128, 128], BF16)
    mleft_sb, mleft_b = sb("mleftt", [128, 128], BF16)
    convf_sb, convf_b = sb("convf", [128, 32], F32)
    cst_sb, cst_b = sb("cst", [128, 4], F32)
    onesc_sb, onesc_b = sb("onesc", [128, 1], F32)
    onesr_sb, onesr_b = sb("onesr", [1, 128], F32)
    gcol_sb, gcol_b = sb("gcol", [128, 268], F32)
    gtr_sb, gtr_b = sb("gtrash", [128, 1024], BF16)
    grec_sb, grec_b = sb("grec", [128, 4], F32)
    anchor_sb, anchor_b = sb("anchor", [1, 1], F32)
    stat_sb, stat_b = sb("stat", [128, 8], F32)

    pool_t = [None, None]
    pool_t_b = [None, None]
    pool_n = [0]

    def bind_pool(scope_es):
        pool_n[0] += 1
        for i in range(2):
            pool_t[i] = scope_es.enter_context(
                nc.psum_tensor(f"pool{pool_n[0]}_{i}", [128, 512], F32))
            pool_t_b[i] = Buf(f"pool{pool_n[0]}_{i}", psum=True)

    for ap_sb, b, src in ((ropec_sb, ropec_b, rope_c), (ropes_sb, ropes_b, rope_s),
                          (ident_sb, ident_b, identt), (perm_sb, perm_b, perm64t),
                          (mdiag_sb, mdiag_b, mdiag),
                          (mleft_sb, mleft_b, mleft), (convf_sb, convf_b, conv_f),
                          (cst_sb, cst_b, cs_tab),
                          (onesc_sb, onesc_b, ones_col), (onesr_sb, onesr_b, ones_row)):
        P.dma("sync", ap_sb[:], src[:], b, writes=(b,))
    P.dma("sync", anchor_sb[:], cs_tab[0:1, 0:1], anchor_b, writes=(anchor_b,))

    def emit_tr_batch(par, items):
        n = len(items)
        for idx, (src_ap, src_b) in enumerate(items):
            P.op("tensor",
                 lambda E, pt=pool_t[par], x=idx, a=src_ap, st=(idx == 0),
                        sp=(idx == n - 1): E.matmul(
                     pt[:, x * 128:(x + 1) * 128], a, ident_sb[:],
                     start=st, stop=sp, skip_group_check=True),
                 reads=(src_b, ident_b), writes=(pool_t_b[par],))

    gamma_state = {"idx": 0}
    GAMMA_PIECES = []
    for wi, wap in ((0, w1), (1, w2)):
        for j in range(NJ):
            for hf in range(2):
                GAMMA_PIECES.append((wi, wap[j * 128:(j + 1) * 128,
                                             hf * 1024:(hf + 1) * 1024]))
    w3flat = w3.rearrange("a b -> (a b)").rearrange("(t p q) -> t p q", p=128, q=1024)
    for tsl in range(2 * NJ):
        GAMMA_PIECES.append((2, w3flat[tsl]))

    def make_gamma_stepper(wq2):
        def step(n):
            for _ in range(n):
                i = gamma_state["idx"]
                if i >= len(GAMMA_PIECES):
                    return
                gamma_state["idx"] += 1
                wi, src = GAMMA_PIECES[i]
                wq_sb, wq_b = wq2[i % 3]
                P.dma("sync" if i % 2 == 0 else "gpsimd", wq_sb[:], src, wq_b,
                      writes=(wq_b,))
                P.op("scalar",
                     lambda E, s_=wq_sb, col=i: E.activation(
                         gtr_sb[:], s_[:], AF.Abs,
                         accum_out=gcol_sb[:, col:col + 1]),
                     reads=(wq_b,), writes=(gtr_b, gcol_b))
        return step

    def gamma_finalize():
        ps_g_cm = nc.psum_tensor("ps_g", [128, 4], F32)
        ps_g = ps_g_cm.__enter__()
        ps_g_b = Buf("ps_g", psum=True)
        for wi in range(3):
            P.op("vector",
                 lambda E, a=wi * 2 * NJ, bnd=(wi + 1) * 2 * NJ, col=wi: E.reduce_sum(
                     stat_sb[:, col:col + 1], gcol_sb[:, a:bnd], axis=AX.X),
                 reads=(gcol_b,), writes=(stat_b,))
        P.op("tensor",
             lambda E: E.matmul(ps_g[0:1, 0:3], onesc_sb[:], stat_sb[:, 0:3],
                                start=True, stop=True, skip_group_check=True),
             reads=(onesc_b, stat_b), writes=(ps_g_b,))
        P.op("vector",
             lambda E: E.tensor_scalar(stat_sb[0:1, 4:7], ps_g[0:1, 0:3],
                                       1.0 / NW_ELEMS, 1e-8, ALU.mult, ALU.add),
             reads=(ps_g_b,), writes=(stat_b,))
        P.op("vector",
             lambda E: E.reciprocal(stat_sb[0:1, 0:3], stat_sb[0:1, 4:7]),
             reads=(stat_b,), writes=(stat_b,))
        for wi in range(3):
            P.op("tensor",
                 lambda E, col=wi, st=(wi == 0), sp=(wi == 2): E.matmul(
                     ps_g[:, col:col + 1], onesr_sb[:],
                     stat_sb[0:1, col:col + 1],
                     start=st, stop=sp, skip_group_check=True),
                 reads=(onesr_b, stat_b), writes=(ps_g_b,))
        P.op("vector",
             lambda E: E.tensor_copy(grec_sb[:, 0:3], ps_g[:, 0:3]),
             reads=(ps_g_b,), writes=(grec_b,))
        ps_g_cm.__exit__(None, None, None)

    QUANT_PIECES = []
    for wi, wap, swap in ((0, w1, w1sw), (1, w2, w2sw)):
        for j in range(NJ):
            for qtr in range(4):
                QUANT_PIECES.append(("w12", wi, wap, swap, j, qtr))
    for c in range(NKD):
        for jg in range((NJ + 3) // 4):
            j0 = jg * 4
            jn = min(4, NJ - j0)
            QUANT_PIECES.append(("w3", 2, w3, w3sw, c, (j0, jn)))
    quant_state = {"idx": 0, "par": 0, "pend": []}

    def make_quant_stepper(slots):
        def emit_back(i):
            (wq_sb, wq_b, s1_sb, s1_b, s2_sb, s2_b, qs_sb, qs_b) = slots[i % len(slots)]
            piece = QUANT_PIECES[i]
            kind, wi, wap, swap = piece[0], piece[1], piece[2], piece[3]
            if kind == "w12":
                j, qtr = piece[4], piece[5]
                ncols = 512
            else:
                c3, (j0, jn) = piece[4], piece[5]
                ncols = jn * 128
            nb = ncols // 128
            for b0 in range(0, nb, 4):
                bn = min(4, nb - b0)
                par = quant_state["par"]
                quant_state["par"] ^= 1
                emit_tr_batch(par, [(s1_sb[:, (b0 + q) * 128:(b0 + q + 1) * 128], s1_b)
                                    for q in range(bn)])
                P.op("vector",
                     lambda E, pt=pool_t[par], a=b0, m=bn, qs=qs_sb: E.tensor_copy(
                         qs[:, a * 128:(a + m) * 128],
                         pt[:, 0:m * 128]),
                     reads=(pool_t_b[par],), writes=(qs_b,))
            if kind == "w12":
                k0 = qtr * 4
                out_ap = swap[k0:k0 + 4, :, j, :].rearrange("k p h -> p k h")
                in_ap = qs_sb[:, 0:512].rearrange("p (k h) -> p k h", k=4)
            else:
                out_ap = swap[c3, :, j0:j0 + jn, :]
                in_ap = qs_sb[:, 0:ncols].rearrange("p (j d) -> p j d", j=jn)
            P.dma("scalar", out_ap, in_ap, qs_b, reads=(qs_b,))

        def step(n):
            for _ in range(n):
                i = quant_state["idx"]
                if i >= len(QUANT_PIECES) + 2:
                    return
                if i >= 2 and quant_state["pend"]:
                    emit_back(quant_state["pend"].pop(0))
                if i >= len(QUANT_PIECES):
                    quant_state["idx"] += 1
                    continue
                quant_state["idx"] += 1
                quant_state["pend"].append(i)
                (wq_sb, wq_b, s1_sb, s1_b, s2_sb, s2_b, qs_sb, qs_b) = slots[i % len(slots)]
                piece = QUANT_PIECES[i]
                kind, wi, wap, swap = piece[0], piece[1], piece[2], piece[3]
                if kind == "w12":
                    j, qtr = piece[4], piece[5]
                    src = wap[j * 128:(j + 1) * 128, qtr * 512:(qtr + 1) * 512]
                    ncols = 512
                else:
                    c3, (j0, jn) = piece[4], piece[5]
                    src = wap[c3 * 128:(c3 + 1) * 128, j0 * 128:(j0 + jn) * 128]
                    ncols = jn * 128
                P.dma("sync" if i % 2 == 0 else "gpsimd", wq_sb[:, 0:ncols],
                      src, wq_b, writes=(wq_b,))
                P.op("vector",
                     lambda E, n_=ncols, w_=wi, s1=s1_sb, wq=wq_sb: E.tensor_scalar(
                         s1[:, 0:n_], wq[:, 0:n_], grec_sb[:, w_:w_ + 1], 0.5,
                         ALU.mult, ALU.is_ge),
                     reads=(wq_b, grec_b), writes=(s1_b,))
                P.op("vector",
                     lambda E, n_=ncols, w_=wi, s2=s2_sb, wq=wq_sb: E.tensor_scalar(
                         s2[:, 0:n_], wq[:, 0:n_], grec_sb[:, w_:w_ + 1], -0.5,
                         ALU.mult, ALU.is_le),
                     reads=(wq_b, grec_b), writes=(s2_b,))
                P.op("vector",
                     lambda E, n_=ncols, s1=s1_sb, s2=s2_sb: E.tensor_tensor(
                         s1[:, 0:n_], s1[:, 0:n_], s2[:, 0:n_], ALU.subtract),
                     reads=(s1_b, s2_b), writes=(s1_b,))
        return step

    def emit_rope(ps_ap, ps_b, half, rope_ctx, dst_sb, dst_b):
        (rin_sb, rin_b, t1_sb, t1_b, ps_rot, ps_rot_b) = rope_ctx
        t0 = half * 1024
        P.op("scalar",
             lambda E, pa=ps_ap: E.copy(rin_sb[:, 0:1024], pa),
             reads=(ps_b,), writes=(rin_b,))
        if rope_mode >= 1:
            for jz in range(2):
                P.op("tensor",
                     lambda E, j=jz: E.matmul(
                         ps_rot[:, j * 512:(j + 1) * 512], perm_sb[:],
                         rin_sb[:, j * 512:(j + 1) * 512],
                         start=True, stop=True, skip_group_check=True),
                     reads=(rin_b, perm_b), writes=(ps_rot_b,))
        if rope_mode >= 2:
            P.op("vector",
                 lambda E, t=t0, pa=ps_ap: E.tensor_tensor(
                     t1_sb[:, 0:1024], pa, ropec_sb[:, t:t + 1024], ALU.mult),
                 reads=(ps_b, ropec_b), writes=(t1_b,))
        if rope_mode >= 3:
            P.op("vector",
                 lambda E, t=t0: E.tensor_tensor(
                     dst_sb[:, t:t + 1024], ps_rot[:], ropes_sb[:, t:t + 1024],
                     ALU.mult),
                 reads=(ps_rot_b, ropes_b), writes=(dst_b,))
        if rope_mode >= 4:
            P.op("vector",
                 lambda E, t=t0: E.tensor_tensor(
                     dst_sb[:, t:t + 1024], dst_sb[:, t:t + 1024],
                     t1_sb[:, 0:1024], ALU.add),
                 reads=(t1_b, dst_b), writes=(dst_b,))
        if rope_mode < 3:
            P.op("vector",
                 lambda E, t=t0, pa=ps_ap: E.tensor_tensor(
                     dst_sb[:, t:t + 1024], pa, ropec_sb[:, t:t + 1024], ALU.mult),
                 reads=(ps_b, ropec_b), writes=(dst_b,))

    def emit_ln_tm(src_ap, src_b, out_ap, out_b, scol):
        P.op("vector",
             lambda E, a=src_ap: E.reduce_sum(stat_sb[:, scol:scol + 1], a,
                                              axis=AX.X),
             reads=(src_b,), writes=(stat_b,))
        P.op("scalar",
             lambda E, a=src_ap, o=out_ap: E.activation(
                 o, a, AF.Square,
                 accum_out=stat_sb[:, scol + 1:scol + 2]),
             reads=(src_b,), writes=(out_b, stat_b))
        P.op("vector",
             lambda E: E.tensor_scalar(stat_sb[:, scol:scol + 1],
                                       stat_sb[:, scol:scol + 1],
                                       1.0 / H, None, ALU.mult),
             reads=(stat_b,), writes=(stat_b,))
        P.op("vector",
             lambda E: E.tensor_scalar(stat_sb[:, scol + 1:scol + 2],
                                       stat_sb[:, scol + 1:scol + 2],
                                       1.0 / H, 1e-5, ALU.mult, ALU.add),
             reads=(stat_b,), writes=(stat_b,))
        P.op("vector",
             lambda E: E.tensor_tensor(stat_sb[:, scol + 2:scol + 3],
                                       stat_sb[:, scol:scol + 1],
                                       stat_sb[:, scol:scol + 1], ALU.mult),
             reads=(stat_b,), writes=(stat_b,))
        P.op("vector",
             lambda E: E.tensor_tensor(stat_sb[:, scol + 1:scol + 2],
                                       stat_sb[:, scol + 1:scol + 2],
                                       stat_sb[:, scol + 2:scol + 3], ALU.subtract),
             reads=(stat_b,), writes=(stat_b,))
        P.op("vector",
             lambda E: E.reciprocal(stat_sb[:, scol + 2:scol + 3],
                                    stat_sb[:, scol + 1:scol + 2]),
             reads=(stat_b,), writes=(stat_b,))
        P.op("scalar",
             lambda E: E.activation(stat_sb[:, scol + 3:scol + 4],
                                    stat_sb[:, scol + 2:scol + 3], AF.Sqrt),
             reads=(stat_b,), writes=(stat_b,))
        P.op("vector",
             lambda E, a=src_ap, o=out_ap: E.tensor_scalar(
                 o, a, stat_sb[:, scol:scol + 1], stat_sb[:, scol + 3:scol + 4],
                 ALU.subtract, ALU.mult),
             reads=(src_b, stat_b), writes=(out_b,))

    quant_step = lambda n: None
    gamma_step = lambda n: None

    for s in range(lim_seqs):
        seq_es = ExitStack()
        kcache_sb = seq_es.enter_context(
            nc.sbuf_tensor(f"kcache{s}", [128, NKV * S], BF16))
        kcache_b = Buf(f"kcache{s}")
        vaug_sb = seq_es.enter_context(
            nc.sbuf_tensor(f"vaug{s}", [128, NKV * 16 * 129], BF16))
        vaug_b = Buf(f"vaug{s}")
        vaug_v = vaug_sb[:].rearrange("p (sl c) -> p sl c", c=129)
        if a_stop >= 1:
            P.op("vector",
                 lambda E, v=vaug_v: E.memset(v[:, :, 128:129], 1.0),
                 writes=(vaug_b,))

        if s == 0 and do_gq:
            gq2 = [(seq_es.enter_context(nc.sbuf_tensor(f"gwq{i}", [128, 1024], F32)),
                    Buf(f"gwq{i}")) for i in range(3)]
            gamma_step = make_gamma_stepper(gq2)
            quant_step = lambda n: None
        elif do_gq:
            qwq3 = [(seq_es.enter_context(nc.sbuf_tensor(f"qwq{i}", [128, 512], F32)),
                     Buf(f"qwq{i}")) for i in range(3)]
            qs13 = [(seq_es.enter_context(nc.sbuf_tensor(f"qs1_{i}", [128, 512], BF16)),
                     Buf(f"qs1_{i}")) for i in range(3)]
            qs2_sb = seq_es.enter_context(nc.sbuf_tensor("qs2s", [128, 512], BF16))
            qs2_bf = Buf("qs2s")
            qst3 = [(seq_es.enter_context(nc.sbuf_tensor(f"qst{i}", [128, 512], BF16)),
                     Buf(f"qst{i}")) for i in range(2)]
            slots = []
            for i in range(6):
                slots.append((qwq3[i % 3][0], qwq3[i % 3][1],
                              qs13[i % 3][0], qs13[i % 3][1],
                              qs2_sb, qs2_bf,
                              qst3[i % 2][0], qst3[i % 2][1]))
            quant_step = make_quant_stepper(slots)
            gamma_step = lambda n: None

        a_es = ExitStack()
        xst_sb, xst_b = [], []
        for i in range(2):
            t = a_es.enter_context(nc.sbuf_tensor(f"xst{s}_{i}", [128, 2048], BF16))
            xst_sb.append(t)
            xst_b.append(Buf(f"xst{s}_{i}"))
        xfm_sb = a_es.enter_context(nc.sbuf_tensor(f"xfm{s}", [128, NKD * 2048], BF16))
        xfm_b = Buf(f"xfm{s}")
        wp_sb, wp_b = [], []
        for i in range(2):
            t = a_es.enter_context(nc.sbuf_tensor(f"wp{s}_{i}", [128, 2048], BF16))
            wp_sb.append(t)
            wp_b.append(Buf(f"wp{s}_{i}"))
        wpf_sb, wpf_b = [], []
        for i in range(2):
            t = a_es.enter_context(nc.sbuf_tensor(f"wpf{s}_{i}", [128, 2048], F32))
            wpf_sb.append(t)
            wpf_b.append(Buf(f"wpf{s}_{i}"))
        rin_sb = xst_sb[0][:, 0:1024]
        rin_b = xst_b[0]
        t1_sb = xst_sb[1][:, 0:1024]
        t1_b = xst_b[1]
        qro0 = a_es.enter_context(nc.sbuf_tensor(f"qro{s}_0", [128, 2048], BF16))
        qro_b0 = Buf(f"qro{s}_0")
        kst_sb = a_es.enter_context(nc.sbuf_tensor(f"kst{s}", [128, 2048], BF16))
        kst_b = Buf(f"kst{s}")
        qro_sb2 = [qro0, kst_sb]
        qro_b2 = [qro_b0, kst_b]
        ctmp_sb = a_es.enter_context(nc.sbuf_tensor(f"ctmp{s}", [128, 2048], BF16))
        ctmp_b = Buf(f"ctmp{s}")
        vst_sb = qro0
        vst_b = qro_b0
        bind_pool(a_es)

        c_order = list(range(16, 24)) + list(range(24, 32)) + list(range(0, 16))
        if a_stop < 3:
            c_order = []
        elif a_stop == 3:
            c_order = c_order[:lim_c]
        wp_r = wpack.rearrange("(k p) c -> p k c", p=128)

        def emit_wpf_load(ci):
            if ci >= len(c_order):
                return
            c = c_order[ci]
            P.dma("sync",
                  wpf_sb[ci % 2][:].rearrange("p (k c) -> p k c", k=NKD),
                  wp_r[:, :, c * 128:(c + 1) * 128], wpf_b[ci % 2],
                  writes=(wpf_b[ci % 2],))

        emit_wpf_load(0)
        emit_wpf_load(1)

        for btg in range(0 if (a_stop < 2 or no_xb) else 8):
            for b2 in range(2):
                blk = btg * 2 + b2
                P.dma("gpsimd", xst_sb[b2][:],
                      xin[s, blk * 128:(blk + 1) * 128, :], xst_b[b2],
                      writes=(xst_b[b2],))
            for k in range(NKD):
                par = k % 2
                emit_tr_batch(par, [(xst_sb[b2][:, k * 128:(k + 1) * 128], xst_b[b2])
                                    for b2 in range(2)])
                P.op("scalar",
                     lambda E, pt=pool_t[par], k_=k, g=btg, xf=xfm_sb: E.copy(
                         xf[:, k_ * 2048 + g * 256: k_ * 2048 + (g + 1) * 256],
                         pt[:, 0:256]),
                     reads=(pool_t_b[par],), writes=(xfm_b,))
            gamma_step(1)
            quant_step(1)

        qkv_es = ExitStack()
        ps_qkv = [qkv_es.enter_context(
            nc.psum_tensor(f"ps_qkv{s}_{i}", [128, 1024], F32)) for i in range(2)]
        ps_qkv_b = [Buf(f"ps_qkv{s}_{i}", psum=True) for i in range(2)]
        ps_rot = qkv_es.enter_context(nc.psum_tensor(f"ps_rot{s}", [128, 1024], F32))
        ps_rot_b = Buf(f"ps_rot{s}", psum=True)
        rope_ctx = (rin_sb, rin_b, t1_sb, t1_b, ps_rot, ps_rot_b)

        def emit_qkv_evac(c, half, par):
            if c < 16:
                qro_sb, qro_b = qro_sb2[c % 2], qro_b2[c % 2]
                if dis_rope:
                    P.op("scalar",
                         lambda E, pp=ps_qkv[par], h_=half, d_=qro_sb: E.copy(
                             d_[:, h_ * 1024:(h_ + 1) * 1024], pp[:, 0:1024]),
                         reads=(ps_qkv_b[par],), writes=(qro_b,))
                else:
                    emit_rope(ps_qkv[par][:, 0:1024], ps_qkv_b[par], half,
                              rope_ctx, qro_sb, qro_b)
            elif c < 24:
                if dis_rope:
                    P.op("scalar",
                         lambda E, pp=ps_qkv[par], h_=half, d_=kst_sb: E.copy(
                             d_[:, h_ * 1024:(h_ + 1) * 1024], pp[:, 0:1024]),
                         reads=(ps_qkv_b[par],), writes=(kst_b,))
                else:
                    emit_rope(ps_qkv[par][:, 0:1024], ps_qkv_b[par], half,
                              rope_ctx, kst_sb, kst_b)
            else:
                P.op("scalar",
                     lambda E, pp=ps_qkv[par], h_=half, d_=vst_sb: E.copy(
                         d_[:, h_ * 1024:(h_ + 1) * 1024], pp[:, 0:1024]),
                     reads=(ps_qkv_b[par],), writes=(vst_b,))
            if half == 1:
                emit_qkv_tail(c)

        def emit_qkv_tail(c):
            if c < 16:
                P.dma("scalar", qscr[s, c], qro_sb2[c % 2][:], qro_b2[c % 2],
                      reads=(qro_b2[c % 2],))
            elif c < 24:
                h = c - 16
                kc = kcache_sb[:, h * S:(h + 1) * S]
                if dis_conv:
                    P.op("vector",
                         lambda E, kc_=kc, ks=kst_sb: E.tensor_copy(kc_, ks[:]),
                         reads=(kst_b,), writes=(kcache_b,))
                elif True:
                    pass
                if not dis_conv:
                    P.op("vector",
                     lambda E, kc_=kc, h_=h, ks=kst_sb: E.tensor_scalar(
                         kc_, ks[:], convf_sb[:, 2 * h_ + 1:2 * h_ + 2],
                         None, ALU.mult),
                     reads=(kst_b, convf_b), writes=(kcache_b,))
                P.op("vector",
                     lambda E, h_=h, ks=kst_sb, ct=ctmp_sb: E.tensor_scalar(
                         ct[:, 0:S - 1], ks[:, 0:S - 1],
                         convf_sb[:, 2 * h_:2 * h_ + 1], None, ALU.mult),
                     reads=(kst_b, convf_b), writes=(ctmp_b,))
                P.op("vector",
                     lambda E, kc_=kc, ct=ctmp_sb: E.tensor_tensor(
                         kc_[:, 1:S], kc_[:, 1:S], ct[:, 0:S - 1], ALU.add),
                     reads=(ctmp_b, kcache_b), writes=(kcache_b,))
            else:
                h = c - 24
                sl0 = h * 16
                P.op("vector",
                     lambda E, h_=h, a=sl0, vv=vaug_v, vs=vst_sb: E.tensor_scalar(
                         vv[:, a:a + 16, 0:128],
                         vs[:].rearrange("p (sl c) -> p sl c", c=128),
                         convf_sb[:, 16 + 2 * h_ + 1:16 + 2 * h_ + 2], None,
                         ALU.mult),
                     reads=(vst_b, convf_b), writes=(vaug_b,))
                P.op("vector",
                     lambda E, h_=h, ct=ctmp_sb, vs=vst_sb: E.tensor_scalar(
                         ct[:], vs[:],
                         convf_sb[:, 16 + 2 * h_:16 + 2 * h_ + 1], None, ALU.mult),
                     reads=(vst_b, convf_b), writes=(ctmp_b,))
                P.op("vector",
                     lambda E, a=sl0, vv=vaug_v, ct=ctmp_sb: E.tensor_tensor(
                         vv[:, a:a + 16, 1:128], vv[:, a:a + 16, 1:128],
                         ct[:].rearrange("p (sl c) -> p sl c", c=128)[:, :, 0:127],
                         ALU.add),
                     reads=(ctmp_b, vaug_b), writes=(vaug_b,))
                P.op("vector",
                     lambda E, a=sl0, vv=vaug_v, ct=ctmp_sb: E.tensor_tensor(
                         vv[:, a + 1:a + 16, 0:1], vv[:, a + 1:a + 16, 0:1],
                         ct[:, 127:2047].rearrange(
                             "p (sl c) -> p sl c", c=128)[:, :, 0:1],
                         ALU.add),
                     reads=(ctmp_b, vaug_b), writes=(vaug_b,))

        pending_evac = [None]
        for ci, c in enumerate(c_order):
            wpar = ci % 2
            P.op("scalar",
                 lambda E, d_=wp_sb[wpar], sr=wpf_sb[wpar]: E.copy(d_[:], sr[:]),
                 reads=(wpf_b[wpar],), writes=(wp_b[wpar],))
            emit_wpf_load(ci + 2)
            for half in range(2):
                par = half
                for k in range(NKD):
                    for js in range(2):
                        P.op("tensor",
                             lambda E, pp=ps_qkv[par], wpp=wp_sb[wpar], k_=k,
                                    h_=half, j_=js, st=(k == 0), xf=xfm_sb,
                                    sp=(k == NKD - 1): E.matmul(
                                 pp[:, j_ * 512:(j_ + 1) * 512],
                                 wpp[:, k_ * 128:(k_ + 1) * 128],
                                 xf[:, k_ * 2048 + h_ * 1024 + j_ * 512:
                                    k_ * 2048 + h_ * 1024 + (j_ + 1) * 512],
                                 start=st, stop=sp, skip_group_check=True),
                             reads=(wp_b[wpar], xfm_b), writes=(ps_qkv_b[par],))
                if pending_evac[0] is not None:
                    pc, ph, ppar = pending_evac[0]
                    emit_qkv_evac(pc, ph, ppar)
                pending_evac[0] = (c, half, par)
            gamma_step(7)
            quant_step(16)
        if pending_evac[0] is not None:
            pc, ph, ppar = pending_evac[0]
            emit_qkv_evac(pc, ph, ppar)
        qkv_es.close()

        for g4 in range(32 if a_stop >= 5 else 0):
            par = g4 % 2
            emit_tr_batch(par, [
                (vaug_v[:, g4 * 4 + q:g4 * 4 + q + 1, 0:128]
                 .rearrange("p a c -> p (a c)"), vaug_b) for q in range(4)])
            P.op("vector",
                 lambda E, pt=pool_t[par], a=g4 * 4, vv=vaug_v: E.tensor_copy(
                     vv[:, a:a + 4, 0:128],
                     pt[:].rearrange("p (a c) -> p a c", c=128)),
                 reads=(pool_t_b[par],), writes=(vaug_b,))
        if s == SEQ - 1 and do_gq:
            quant_step(len(QUANT_PIECES) + 4)
            assert quant_state["idx"] >= len(QUANT_PIECES)
            assert not quant_state["pend"], quant_state["pend"]
        if debug_taps:
            P.dma("gpsimd", kdbg[s], kcache_sb[:], kcache_b, reads=(kcache_b,))
            P.dma("gpsimd", vdbg[s], vaug_sb[:], vaug_b, reads=(vaug_b,))
        a_es.close()
        P.barrier(anchor_sb[:])

        if s == 0 and do_gq:
            gamma_step(len(GAMMA_PIECES))
            gamma_finalize()
            P.barrier(anchor_sb[:])

        b_es = ExitStack()
        wo_sb = b_es.enter_context(nc.sbuf_tensor(f"wo_sb{s}", [128, NKD * 2048], BF16))
        wo_b = Buf(f"wo_sb{s}")
        qsb_sb, qsb_b, xr_sb, xr_b = [], [], [], []
        for i in range(2):
            t = b_es.enter_context(nc.sbuf_tensor(f"qsb{s}_{i}", [128, 2048], BF16))
            qsb_sb.append(t)
            qsb_b.append(Buf(f"qsb{s}_{i}"))
            t = b_es.enter_context(nc.sbuf_tensor(f"xr{s}_{i}", [128, 2048], F32))
            xr_sb.append(t)
            xr_b.append(Buf(f"xr{s}_{i}"))
        pT_sb, pT_b = [], []
        for i in range(6):
            t = b_es.enter_context(nc.sbuf_tensor(f"pT{s}_{i}", [128, 512], BF16))
            pT_sb.append(t)
            pT_b.append(Buf(f"pT{s}_{i}"))
        atm_sb, atm_b = [], []
        for i in range(3):
            t = b_es.enter_context(nc.sbuf_tensor(f"atm{s}_{i}", [128, 128], BF16))
            atm_sb.append(t)
            atm_b.append(Buf(f"atm{s}_{i}"))
        afm_sb = b_es.enter_context(nc.sbuf_tensor(f"afm{s}", [128, 2048], BF16))
        afm_b = Buf(f"afm{s}")
        ysb_sb = b_es.enter_context(nc.sbuf_tensor(f"ysb{s}", [128, 2048], F32))
        ysb_b = Buf(f"ysb{s}")
        yn_sb = b_es.enter_context(nc.sbuf_tensor(f"yn{s}", [128, 2048], BF16))
        yn_b = Buf(f"yn{s}")
        rcol_sb = stat_sb[:, 6:8]
        rcol_b = stat_b

        ps_b_es = ExitStack()
        NSS = 2 if b_old else 4
        ps_s = [ps_b_es.enter_context(nc.psum_tensor(f"ps_s{s}_{i}", [128, 512], F32))
                for i in range(NSS)]
        ps_s_b = [Buf(f"ps_s{s}_{i}", psum=True) for i in range(NSS)]
        if b_old:
            bind_pool(ps_b_es)
        ps_o = [ps_b_es.enter_context(nc.psum_tensor(f"ps_o{s}_{i}", [128, 129], F32))
                for i in range(2)]
        ps_o_b = [Buf(f"ps_o{s}_{i}", psum=True) for i in range(2)]
        ps_y = [ps_b_es.enter_context(nc.psum_tensor(f"ps_y{s}_{i}", [128, 512], F32))
                for i in range(2)]
        ps_y_b = [Buf(f"ps_y{s}_{i}", psum=True) for i in range(2)]

        wo_r = wo.rearrange("(k p) d -> p k d", p=128)
        P.dma("gpsimd", wo_sb[:].rearrange("p (k d) -> p k d", k=NKD),
              wo_r, wo_b, writes=(wo_b,))

        qscr_r = qscr[s].rearrange("h p t -> p h t")
        sgrp = 0
        for i in range(lim_blk):
            ipar = i % 2
            P.dma("sync",
                  qsb_sb[ipar][:].rearrange("p (h t) -> p h t", h=NH),
                  qscr_r[:, :, i * 128:(i + 1) * 128], qsb_b[ipar],
                  writes=(qsb_b[ipar],))
            P.dma("sync", xr_sb[ipar][:], xin[s, i * 128:(i + 1) * 128, :],
                  xr_b[ipar], writes=(xr_b[ipar],))
            kt_lo = max(0, i - KTW)
            kts = list(range(kt_lo, i + 1))
            head_groups = {}

            def emit_scores(h):
                nonlocal sgrp
                kv = h // 2
                glist = []
                for g0 in range(0, len(kts), 4):
                    grp = kts[g0:g0 + 4]
                    spar = sgrp % 6
                    sspar = sgrp % NSS
                    sgrp += 1
                    glist.append((spar, grp))
                    mmseq = []
                    for idx, kt in enumerate(grp):
                        mmseq.append((idx, kt, False))
                        if kt == i:
                            mmseq.append((idx, kt, "diag"))
                        elif kt == i - KTW:
                            mmseq.append((idx, kt, "left"))
                    nmm = len(mmseq)
                    for mi, (idx, kt, mk) in enumerate(mmseq):
                        st = (mi == 0)
                        sp = (mi == nmm - 1)
                        if mk is False:
                            P.op("tensor",
                                 lambda E, pp=ps_s[sspar], x=idx, kv_=kv, kt_=kt,
                                        h_=h, kcs=kcache_sb, qs=qsb_sb[ipar],
                                        st_=st, sp_=sp: E.matmul(
                                     pp[:, x * 128:(x + 1) * 128],
                                     kcs[:, kv_ * S + kt_ * 128:
                                         kv_ * S + (kt_ + 1) * 128],
                                     qs[:, h_ * 128:(h_ + 1) * 128],
                                     start=st_, stop=sp_, skip_group_check=True),
                                 reads=(kcache_b, qsb_b[ipar]),
                                 writes=(ps_s_b[sspar],))
                        else:
                            msk_sb = mdiag_sb if mk == "diag" else mleft_sb
                            msk_b = mdiag_b if mk == "diag" else mleft_b
                            P.op("tensor",
                                 lambda E, pp=ps_s[sspar], x=idx, m_=msk_sb,
                                        st_=st, sp_=sp: E.matmul(
                                     pp[:, x * 128:(x + 1) * 128],
                                     m_[:], ident_sb[:],
                                     start=st_, stop=sp_, skip_group_check=True),
                                 reads=(msk_b, ident_b), writes=(ps_s_b[sspar],))
                    n = len(grp) * 128
                    P.op("scalar",
                         lambda E, pt=pT_sb[spar], pp=ps_s[sspar], n_=n:
                         E.activation(pt[:, 0:n_], pp[:, 0:n_], AF.Exp,
                                      scale=INV_SQRT_HD),
                         reads=(ps_s_b[sspar],), writes=(pT_b[spar],))
                head_groups[h] = glist

            def emit_attnv(h):
                kv = h // 2
                opar = h % 2
                for spar, grp in head_groups[h]:
                    for idx, kt in enumerate(grp):
                        P.op("tensor",
                             lambda E, po=ps_o[opar], pt=pT_sb[spar], x=idx,
                                    kv_=kv, kt_=kt, vv=vaug_v,
                                    st=(kt == kts[0]), sp=(kt == kts[-1]): E.matmul(
                                 po[:],
                                 pt[:, x * 128:(x + 1) * 128],
                                 vv[:, kv_ * 16 + kt_:kv_ * 16 + kt_ + 1, :]
                                 .rearrange("p a c -> p (a c)"),
                                 start=st, stop=sp, skip_group_check=True),
                             reads=(pT_b[spar], vaug_b), writes=(ps_o_b[opar],))
                P.op("vector",
                     lambda E, o_=opar, rl=rcol_sb, po=ps_o[opar]: E.reciprocal(
                         rl[:, o_:o_ + 1], po[:, 128:129]),
                     reads=(ps_o_b[opar],), writes=(rcol_b,))
                tpar = h % 3
                P.op("scalar",
                     lambda E, o_=opar, at=atm_sb[tpar], po=ps_o[opar],
                            rl=rcol_sb: E.activation(
                         at[:], po[:, 0:128], AF.Copy,
                         scale=rl[:, o_:o_ + 1]),
                     reads=(ps_o_b[opar], rcol_b), writes=(atm_b[tpar],))

            def emit_attn_tr(h):
                tpar = h % 3
                opar = h % 2
                if b_old:
                    P.op("tensor",
                         lambda E, pt=pool_t[opar], at=atm_sb[tpar]: E.matmul(
                             pt[:, 0:128], at[:], ident_sb[:],
                             start=True, stop=True, skip_group_check=True),
                         reads=(atm_b[tpar], ident_b), writes=(pool_t_b[opar],))
                    P.op("scalar",
                         lambda E, pt=pool_t[opar], h_=h, af=afm_sb: E.copy(
                             af[:, h_ * 128:(h_ + 1) * 128], pt[:, 0:128]),
                         reads=(pool_t_b[opar],), writes=(afm_b,))
                    return
                P.op("tensor",
                     lambda E, py=ps_y[opar], at=atm_sb[tpar]: E.matmul(
                         py[:, 0:128], at[:], ident_sb[:],
                         start=True, stop=True, skip_group_check=True),
                     reads=(atm_b[tpar], ident_b), writes=(ps_y_b[opar],))
                P.op("vector",
                     lambda E, py=ps_y[opar], h_=h, af=afm_sb: E.tensor_copy(
                         af[:, h_ * 128:(h_ + 1) * 128], py[:, 0:128]),
                     reads=(ps_y_b[opar],), writes=(afm_b,))

            for h in range(NH):
                emit_scores(h)
                if h >= 1:
                    emit_attnv(h - 1)
                if h >= 2:
                    emit_attn_tr(h - 2)
            emit_attnv(NH - 1)
            emit_attn_tr(NH - 2)
            emit_attn_tr(NH - 1)
            for js in range(4):
                ypar = js % 2
                for cch in range(NKD):
                    P.op("tensor",
                         lambda E, py=ps_y[ypar], c_=cch, j_=js, st=(cch == 0),
                                af=afm_sb, wos=wo_sb,
                                sp=(cch == NKD - 1): E.matmul(
                             py[:], af[:, c_ * 128:(c_ + 1) * 128],
                             wos[:, c_ * 2048 + j_ * 512:
                                 c_ * 2048 + (j_ + 1) * 512],
                             start=st, stop=sp, skip_group_check=True),
                         reads=(afm_b, wo_b), writes=(ps_y_b[ypar],))
                P.op("vector",
                     lambda E, py=ps_y[ypar], j_=js, ys=ysb_sb,
                            xr=xr_sb[ipar]: E.tensor_tensor(
                         ys[:, j_ * 512:(j_ + 1) * 512], py[:],
                         xr[:, j_ * 512:(j_ + 1) * 512], ALU.add),
                     reads=(ps_y_b[ypar], xr_b[ipar]), writes=(ysb_b,))
            emit_ln_tm(ysb_sb[:], ysb_b, yn_sb[:], yn_b, 0)
            P.dma("gpsimd", yscr[(s * S + i * 128):(s * S + (i + 1) * 128), :],
                  yn_sb[:], yn_b, reads=(yn_b,))
        ps_b_es.close()
        b_es.close()
        seq_es.close()
        P.barrier(anchor_sb[:])

    if do_gq:
        assert gamma_state["idx"] >= len(GAMMA_PIECES)
    P.barrier(anchor_sb[:])

    c_es = ExitStack()
    yfm_sb = c_es.enter_context(nc.sbuf_tensor("yfm", [128, NKD * GT], BF16))
    yfm_b = Buf("yfm")
    h_sb = c_es.enter_context(nc.sbuf_tensor("hsb", [128, NJ * GT], BF16))
    h_b = Buf("hsb")
    ytm_sb, ytm_b = [], []
    for i in range(4):
        t = c_es.enter_context(nc.sbuf_tensor(f"ytm{i}", [128, 2048], BF16))
        ytm_sb.append(t)
        ytm_b.append(Buf(f"ytm{i}"))
    w1t_sb, w1t_b, w2t_sb, w2t_b = [], [], [], []
    for i in range(2):
        t = c_es.enter_context(nc.sbuf_tensor(f"w1t{i}", [128, NKD * 256], BF16))
        w1t_sb.append(t)
        w1t_b.append(Buf(f"w1t{i}"))
        t = c_es.enter_context(nc.sbuf_tensor(f"w2t{i}", [128, NKD * 256], BF16))
        w2t_sb.append(t)
        w2t_b.append(Buf(f"w2t{i}"))
    w3t_sb, w3t_b = [], []
    for i in range(2):
        t = c_es.enter_context(nc.sbuf_tensor(f"w3t{i}", [128, NJ * 128], BF16))
        w3t_sb.append(t)
        w3t_b.append(Buf(f"w3t{i}"))
    sg_sb = c_es.enter_context(nc.sbuf_tensor("sg", [128, GT], F32))
    sg_b = Buf("sg")
    ob_sb = c_es.enter_context(nc.sbuf_tensor("ob", [128, GT], BF16))
    ob_b = Buf("ob")
    fm_sb = c_es.enter_context(nc.sbuf_tensor("fm", [128, GT], BF16))
    fm_b = Buf("fm")
    tm_sb = c_es.enter_context(nc.sbuf_tensor("tm", [128, NTT * 2048], BF16))
    tm_b = Buf("tm")
    of_sb = c_es.enter_context(nc.sbuf_tensor("of", [128, 2048], F32))
    of_b = Buf("of")
    bind_pool(c_es)

    ps_c_es = ExitStack()
    ps_x1 = [ps_c_es.enter_context(nc.psum_tensor(f"ps_x1{i}", [128, GT], F32))
             for i in range(2)]
    ps_x1_b = [Buf(f"ps_x1{i}", psum=True) for i in range(2)]
    ps_x2 = [ps_c_es.enter_context(nc.psum_tensor(f"ps_x2{i}", [128, GT], F32))
             for i in range(2)]
    ps_x2_b = [Buf(f"ps_x2{i}", psum=True) for i in range(2)]
    ps_of = [ps_c_es.enter_context(nc.psum_tensor(f"ps_of{i}", [128, GT], F32))
             for i in range(2)]
    ps_of_b = [Buf(f"ps_of{i}", psum=True) for i in range(2)]

    tm_v = tm_sb[:].rearrange("p (t d) -> p t d", t=NTT)
    for g in range(NG if do_c else 0):
        gs = g // (S // GT)
        grow = g * GT
        for bt in range(NTT):
            P.dma("sync", ytm_sb[bt][:],
                  yscr[grow + bt * 128: grow + (bt + 1) * 128, :],
                  ytm_b[bt], writes=(ytm_b[bt],))
        for k in range(NKD):
            par = k % 2
            emit_tr_batch(par, [(ytm_sb[bt][:, k * 128:(k + 1) * 128], ytm_b[bt])
                                for bt in range(NTT)])
            P.op("vector",
                 lambda E, pt=pool_t[par], k_=k: E.tensor_copy(
                     yfm_sb[:, k_ * GT:(k_ + 1) * GT], pt[:]),
                 reads=(pool_t_b[par],), writes=(yfm_b,))
        for jp in range(NJ // 2):
            wpar = jp % 2
            P.dma("sync",
                  w1t_sb[wpar][:].rearrange("p (k j h) -> p k j h", k=NKD, j=2),
                  w1sw[:, :, jp * 2:(jp + 1) * 2, :].rearrange("k p j h -> p k j h"),
                  w1t_b[wpar], writes=(w1t_b[wpar],))
            P.dma("sync",
                  w2t_sb[wpar][:].rearrange("p (k j h) -> p k j h", k=NKD, j=2),
                  w2sw[:, :, jp * 2:(jp + 1) * 2, :].rearrange("k p j h -> p k j h"),
                  w2t_b[wpar], writes=(w2t_b[wpar],))
            for jj in range(2):
                j = jp * 2 + jj
                xpar = j % 2
                for k in range(NKD):
                    P.op("tensor",
                         lambda E, p=xpar, wp_=wpar, k_=k, jj_=jj, st=(k == 0),
                                sp=(k == NKD - 1): E.matmul(
                             ps_x1[p][:],
                             w1t_sb[wp_][:, (k_ * 2 + jj_) * 128:
                                          (k_ * 2 + jj_ + 1) * 128],
                             yfm_sb[:, k_ * GT:(k_ + 1) * GT],
                             start=st, stop=sp, skip_group_check=True),
                         reads=(w1t_b[wpar], yfm_b), writes=(ps_x1_b[xpar],))
                for k in range(NKD):
                    P.op("tensor",
                         lambda E, p=xpar, wp_=wpar, k_=k, jj_=jj, st=(k == 0),
                                sp=(k == NKD - 1): E.matmul(
                             ps_x2[p][:],
                             w2t_sb[wp_][:, (k_ * 2 + jj_) * 128:
                                          (k_ * 2 + jj_ + 1) * 128],
                             yfm_sb[:, k_ * GT:(k_ + 1) * GT],
                             start=st, stop=sp, skip_group_check=True),
                         reads=(w2t_b[wpar], yfm_b), writes=(ps_x2_b[xpar],))
                P.op("scalar",
                     lambda E, p=xpar: E.activation(
                         sg_sb[:], ps_x1[p][:], AF.Sigmoid),
                     reads=(ps_x1_b[xpar],), writes=(sg_b,))
                P.op("vector",
                     lambda E, p=xpar, j_=j: E.tensor_tensor(
                         h_sb[:, j_ * GT:(j_ + 1) * GT], sg_sb[:], ps_x2[p][:],
                         ALU.mult),
                     reads=(sg_b, ps_x2_b[xpar]), writes=(h_b,))
        def emit_cb_evac(c):
            cpar = c % 2
            P.op("scalar",
                 lambda E, p=cpar: E.copy(ob_sb[:], ps_of[p][:]),
                 reads=(ps_of_b[cpar],), writes=(ob_b,))
            P.op("vector",
                 lambda E, c_=c: E.tensor_tensor(
                     fm_sb[:], ob_sb[:], yfm_sb[:, c_ * GT:(c_ + 1) * GT],
                     ALU.add),
                 reads=(ob_b, yfm_b), writes=(fm_b,))
            tpar = c % 2
            emit_tr_batch(tpar, [(fm_sb[:, tt * 128:(tt + 1) * 128], fm_b)
                                 for tt in range(NTT)])
            P.op("vector",
                 lambda E, pt=pool_t[tpar], c_=c: E.tensor_copy(
                     tm_v[:, :, c_ * 128:(c_ + 1) * 128],
                     pt[:].rearrange("p (t c) -> p t c", c=128)),
                 reads=(pool_t_b[tpar],), writes=(tm_b,))

        pend_cb = [None]
        for c in range(NKD):
            cpar = c % 2
            P.dma("sync", w3t_sb[cpar][:],
                  w3sw[c].rearrange("p j d -> p (j d)"),
                  w3t_b[cpar], writes=(w3t_b[cpar],))
            for j in range(NJ):
                P.op("tensor",
                     lambda E, p=cpar, j_=j, st=(j == 0), sp=(j == NJ - 1): E.matmul(
                         ps_of[p][:], w3t_sb[p][:, j_ * 128:(j_ + 1) * 128],
                         h_sb[:, j_ * GT:(j_ + 1) * GT],
                         start=st, stop=sp, skip_group_check=True),
                     reads=(w3t_b[cpar], h_b), writes=(ps_of_b[cpar],))
            if pend_cb[0] is not None:
                emit_cb_evac(pend_cb[0])
            pend_cb[0] = c
        emit_cb_evac(pend_cb[0])
        for tt in range(NTT):
            emit_ln_tm(tm_sb[:, tt * 2048:(tt + 1) * 2048], tm_b,
                       of_sb[:], of_b, 4)
            r0 = (grow + tt * 128) % S
            P.dma("gpsimd", yout[gs, r0:r0 + 128, :], of_sb[:], of_b,
                  reads=(of_b,))
    ps_c_es.close()
    c_es.close()

    P.barrier(anchor_sb[:])
    P.finalize()

    with nc.Block() as block:
        block.sync(lambda E: P.replay("sync", E))
        block.gpsimd(lambda E: P.replay("gpsimd", E))
        block.scalar(lambda E: P.replay("scalar", E))
        block.vector(lambda E: P.replay("vector", E))
        block.tensor(lambda E: P.replay("tensor", E))

    es.close()
    nc._P = P
    return nc


def kernel(L0_attn_hidden, L0_attn_W_pack, L0_attn_W_o, L0_attn_conv_k,
           L0_attn_conv_v, L0_ffn_w1, L0_ffn_w2, L0_ffn_w3):
    consts = build_host_constants(np.asarray(L0_attn_conv_k),
                                  np.asarray(L0_attn_conv_v))
    nc = build_program()
    shared = {
        "wpack": np.ascontiguousarray(np.asarray(L0_attn_W_pack)),
        "wo": np.ascontiguousarray(np.asarray(L0_attn_W_o)),
        "w1": np.ascontiguousarray(np.asarray(L0_ffn_w1)),
        "w2": np.ascontiguousarray(np.asarray(L0_ffn_w2)),
        "w3": np.ascontiguousarray(np.asarray(L0_ffn_w3)),
        "rope_c": consts["rope_c"], "rope_s": consts["rope_s"],
        "ident": consts["ident"], "perm64": consts["perm64"],
        "mdiag": consts["mdiag"],
        "mleft": consts["mleft"], "conv_f": consts["conv_f"],
        "conv_b": consts["conv_b"], "cs_tab": consts["cs_tab"],
        "ones_col": consts["ones_col"], "ones_row": consts["ones_row"],
    }
    x = np.asarray(L0_attn_hidden)
    in_maps = []
    for c in range(NCORES):
        m = dict(shared)
        m["xin"] = np.ascontiguousarray(x[c * SEQ:(c + 1) * SEQ])
        in_maps.append(m)
    res = run_bass_kernel_spmd(nc, in_maps, core_ids=list(range(NCORES)))
    out = np.concatenate([r["yout"] for r in res.results], axis=0)
    return np.asarray(out, dtype=np.float32)
